# version 08
# baseline (speedup 1.0000x reference)
import contextlib

import numpy as np
import ml_dtypes

import concourse.bass as bass
import concourse.mybir as mybir
from concourse.bass_utils import run_bass_kernel_spmd

F32 = mybir.dt.float32
BF16 = mybir.dt.bfloat16
I32 = mybir.dt.int32
AF = mybir.ActivationFunctionType
ALU = mybir.AluOpType
AX = mybir.AxisListType

H = 4096
S = 4096
B = 4
D = 1024
SOWN = 2048
NCH = 32
GT = 512
NG = 8
TT = 512
NTO = 4
EPS = 1e-5


class Ctrs:
    def __init__(self, nc, names):
        self.h = {}
        self.c = {}
        self.ctx = contextlib.ExitStack()
        for n in names:
            self.h[n] = self.ctx.enter_context(nc.semaphore(n))
            self.c[n] = 0

    def bump(self, n, k=1):
        self.c[n] += k
        return self.c[n]

    def val(self, n):
        return self.c[n]


def build_nc():
    nc = bass.Bass("TRN2", target_bir_lowering=False)

    def din(name, shape, dt=F32):
        return nc.dram_tensor(name, shape, dt, kind="ExternalInput").ap()

    xt0 = din("xt0", [128, 4, NCH, GT])
    xt1 = din("xt1", [128, 4, NCH, GT])
    x_own = din("x_own", [SOWN, H])
    wq = din("wq", [128, NCH, D])
    wk = din("wk", [128, NCH, D])
    wv = din("wv", [128, NCH, D])
    wf = din("wf", [8, 128, NCH, 512])
    wo = din("wo", [8, 128, 8, 512])
    acat = din("acat", [128, NCH, 48])
    bcat = din("bcat", [8, 48, 512])
    bqc_d = din("bqc", [128, 8])
    bkc_d = din("bkc", [128, 8])
    bv_d = din("bv_row", [1, D])
    bo_d = din("bo_row", [1, H])
    bf_d = din("bf_row", [1, H])
    lsi_d = din("lsi", [1, 4], I32)
    bvec_d = din("bvec", [1, 1])
    idxp_d = din("idxp", [1, 48])
    onesf_d = din("onesf", [1, 128])
    onesb_d = din("onesb", [1, 128], BF16)
    ident_d = din("ident", [128, 128], BF16)

    out_d = nc.dram_tensor("out", [SOWN, H], F32, kind="ExternalOutput").ap()

    xns = nc.dram_tensor("xns", [2, 128, 4, NCH, GT], BF16).ap()
    ks = nc.dram_tensor("ks", [128, 8, S], BF16).ap()
    vs = nc.dram_tensor("vs", [128, NCH, D], BF16).ap()
    qs = nc.dram_tensor("qs", [16, 128, 8, 128], BF16).ap()
    ots = nc.dram_tensor("ots", [16, 128, 8, 128], BF16).ap()
    flsh = nc.dram_tensor("flsh", [128, 2], BF16).ap()
    wfbf = nc.dram_tensor("wfbf", [8, 128, NCH, 512], BF16).ap()
    wobf = nc.dram_tensor("wobf", [8, 128, 8, 512], BF16).ap()
    bcbf = nc.dram_tensor("bcbf", [8, 48, 512], BF16).ap()

    sm = Ctrs(nc, ["qld", "qw", "qst", "pe", "dv", "sc", "dvf", "scf"])
    SEM = sm.h

    def fl(eng, name, inst):
        t = sm.bump(name)
        inst.then_inc(SEM[name], 1)
        eng.wait_ge(SEM[name], t)

    def dma(eng, name, out, in_):
        t = sm.bump(name, 16)
        eng.dma_start(out=out, in_=in_).then_inc(SEM[name], 16)
        return t

    top = contextlib.ExitStack()

    def ptop(name, shape, dt):
        return top.enter_context(nc.sbuf_tensor(name, shape, dt, side="right"))

    onesf = ptop("onesf_s", [1, 128], F32)
    onesb = ptop("onesb_s", [1, 128], BF16)
    ident = ptop("ident_s", [128, 128], BF16)
    bqc = ptop("bqc_s", [128, 8], F32)
    bkc = ptop("bkc_s", [128, 8], F32)
    bq32 = ptop("bq32", [128, 8], F32)
    lsii = ptop("lsii_s", [1, 4], I32)
    lsif = ptop("lsif", [1, 4], F32)
    bvec = ptop("bvec_s", [1, 1], F32)
    idxp = ptop("idxp_s", [1, 48], F32)
    ge = ptop("ge", [1, 4], F32)
    sg = ptop("sg", [1, 8], F32)
    mrow = ptop("mrow", [1, 48], F32)
    mcol = ptop("mcol", [48, 1], F32)
    rsb = ptop("rsb", [48, SOWN], BF16)
    btot = ptop("btot", [1, H], BF16)
    ones_col_bf = nc.const_aps.tensor(1.0, [128, 1], BF16)

    WSA = ptop("WSA", [128, 32768], BF16)

    b_qw, b_qld, b_pe, b_dv, b_sc = (sm.val(n) for n in ("qw", "qld", "pe", "dv", "sc"))
    QW_TINY = b_qw + 16 * 8
    QW_G = lambda g: QW_TINY + 16 * (g + 1) + (16 if g >= 1 else 0) + (32 if g >= 4 else 0)
    FLG0 = QW_TINY + 32
    QW_WK = QW_TINY + 16 * 6 + 16
    DV_MROW = b_dv + 1
    DV_MCOL = b_dv + 2
    PE_MASK = b_pe + 1
    PE_ST = lambda g, c: PE_MASK + 32 * g + 33 * max(g - 1, 0) + c + 1
    PE_BC = lambda g: PE_MASK + 32 * (min(g + 1, NG - 1) + 1) + 33 * g + 1
    PE_NORM = lambda g, c: PE_BC(g) + c + 1
    SC_SQ = lambda g, c: b_sc + g * 33 + c + 1
    SC_RSTD = lambda g: b_sc + g * 33 + 33
    DV_VAR = lambda g: DV_MCOL + g * 34 + 1
    DV_RC = lambda g: DV_MCOL + g * 34 + 2
    DV_MULT = lambda g, c: DV_MCOL + g * 34 + 2 + c + 1
    QLD_ST8 = lambda g, sl: b_qld + 16 * (g * 4 + sl + 1)

    with contextlib.ExitStack() as ph, nc.Block(no_gpsimd_drain=True) as blk:
        E = ph.enter_context
        xbg = [E(nc.sbuf_tensor(f"xbg{i}", [128, NCH * GT], BF16)) for i in range(3)]
        xsq = [E(nc.sbuf_tensor(f"xsq{i}", [128, GT], BF16)) for i in range(2)]
        s1row = E(nc.sbuf_tensor("s1row", [1, GT], F32))
        s2row = E(nc.sbuf_tensor("s2row", [1, GT], F32))
        m2row = E(nc.sbuf_tensor("m2row", [1, GT], F32))
        negmb = [E(nc.sbuf_tensor(f"negmb{i}", [1, GT], BF16)) for i in range(2)]
        r128 = [E(nc.sbuf_tensor(f"r128_{i}", [128, GT], F32)) for i in range(2)]
        xnb = [E(nc.sbuf_tensor(f"xnb{i}", [128, 8 * GT], BF16)) for i in range(2)]
        s1p = [E(nc.psum_tensor(f"s1p{i}", [1, GT], F32)) for i in range(2)]
        s2p = [E(nc.psum_tensor(f"s2p{i}", [1, GT], F32)) for i in range(2)]
        pr = E(nc.psum_tensor("pr", [128, GT], F32))
        pn = [E(nc.psum_tensor(f"pn{i}", [128, GT], F32)) for i in range(3)]
        pmask = pn[0]

        xt = [xt0, xt1]

        def gsl(g):
            return g // 4, g % 4

        @blk.gpsimd
        def _(g_):
            dma(g_, "qw", onesf[:, :], onesf_d)
            dma(g_, "qw", onesb[:, :], onesb_d)
            dma(g_, "qw", ident[:, :], ident_d)
            dma(g_, "qw", bqc[:, :], bqc_d)
            dma(g_, "qw", bkc[:, :], bkc_d)
            dma(g_, "qw", lsii[:, :], lsi_d)
            dma(g_, "qw", bvec[:, :], bvec_d)
            dma(g_, "qw", idxp[:, :], idxp_d)
            assert sm.val("qw") == QW_TINY
            for g in range(NG):
                p, gI = gsl(g)
                if g >= 3:
                    g_.wait_ge(SEM["dv"], DV_MULT(g - 3, NCH - 1))
                t = dma(g_, "qw",
                        xbg[g % 3].ap().rearrange("p (c t) -> p c t", t=GT),
                        xt[p][:, gI, :, :])
                if g == 0:
                    assert dma(g_, "qw", flsh, xbg[0][:, 0:2]) == FLG0
                else:
                    assert t == QW_G(g)
                if g == 3:
                    for h in range(2):
                        dma(g_, "qw",
                            WSA.ap().rearrange("p (c o) -> p c o", o=D)[:, h * 16:(h + 1) * 16, :],
                            wk[:, h * 16:(h + 1) * 16, :])
                    assert sm.val("qw") == QW_WK

        @blk.vector
        def _(v):
            v.wait_ge(SEM["qw"], QW_TINY)
            fl(v, "dvf", v.tensor_scalar_mul(bq32[:, :], bqc[:, :], 1.0 / 32.0))
            fl(v, "dvf", v.tensor_copy(lsif[:, :], lsii[:, :]))
            fl(v, "dvf", v.tensor_single_scalar(ge[:, :], lsif[:, :], bvec[:, :], ALU.is_le))
            fl(v, "dvf", v.tensor_scalar_add(sg[:, 0:1], ge[:, 3:4], 3.0))
            fl(v, "dvf", v.tensor_scalar(sg[:, 3:4], ge[:, 1:2], -1.0, 1.0, ALU.mult, ALU.add))
            fl(v, "dvf", v.tensor_mul(sg[:, 3:4], sg[:, 3:4], ge[:, 0:1]))
            fl(v, "dvf", v.tensor_scalar(sg[:, 1:2], ge[:, 1:2], 2.0, None, ALU.mult))
            fl(v, "dvf", v.tensor_add(sg[:, 1:2], sg[:, 1:2], sg[:, 3:4]))
            fl(v, "dvf", v.tensor_mul(sg[:, 0:1], sg[:, 0:1], ge[:, 2:3]))
            fl(v, "dvf", v.tensor_scalar(sg[:, 3:4], ge[:, 2:3], -1.0, 1.0, ALU.mult, ALU.add))
            fl(v, "dvf", v.tensor_mul(sg[:, 1:2], sg[:, 1:2], sg[:, 3:4]))
            fl(v, "dvf", v.tensor_add(sg[:, 0:1], sg[:, 0:1], sg[:, 1:2]))
            fl(v, "dvf", v.tensor_scalar(sg[:, 1:2], sg[:, 0:1], -2.0, 0.0, ALU.add, ALU.max))
            fl(v, "dvf", v.tensor_scalar_min(sg[:, 1:2], sg[:, 1:2], 2.0))
            fl(v, "dvf", v.tensor_single_scalar(sg[:, 2:3], sg[:, 0:1], 1.0, ALU.is_gt))
            fl(v, "dvf", v.tensor_single_scalar(mrow[:, :], idxp[:, :], sg[:, 1:2], ALU.is_equal))
            v.tensor_single_scalar(mrow[:, :], mrow[:, :], sg[:, 2:3], ALU.mult).then_inc(SEM["dv"], 1)
            assert sm.bump("dv") == DV_MROW

        @blk.tensor
        def _(t):
            t.wait_ge(SEM["dv"], DV_MROW)
            t.matmul(pmask[0:48, 0:1], mrow[:, :], onesf[:, 0:1], start=True, stop=True).then_inc(SEM["pe"], 1)
            assert sm.bump("pe") == PE_MASK

        @blk.vector
        def _(v):
            v.wait_ge(SEM["pe"], PE_MASK)
            v.tensor_copy(mcol[:, :], pmask[0:48, 0:1]).then_inc(SEM["dv"], 1)
            assert sm.bump("dv") == DV_MCOL

        @blk.scalar
        def _(s):
            for g in range(NG):
                for c in range(NCH):
                    if c == 0:
                        s.wait_ge(SEM["qw"], FLG0 if g == 0 else QW_G(g))
                    gc = g * NCH + c
                    if gc >= 2:
                        pg, pc = divmod(gc - 2, NCH)
                        s.wait_ge(SEM["pe"], PE_ST(pg, pc))
                    s.square(xsq[c % 2][:, :], xbg[g % 3][:, c * GT:(c + 1) * GT]).then_inc(SEM["sc"], 1)
                    assert sm.bump("sc") == SC_SQ(g, c)
                s.wait_ge(SEM["dv"], DV_VAR(g))
                fl(s, "scf", s.activation(s2row[:, :], s2row[:, :], AF.Ln))
                s.activation(s2row[:, :], s2row[:, :], AF.Exp, scale=-0.5).then_inc(SEM["sc"], 1)
                assert sm.bump("sc") == SC_RSTD(g)

        def emit_stats(t, g):
            t.wait_ge(SEM["qw"], FLG0 if g == 0 else QW_G(g))
            if g >= 2:
                t.wait_ge(SEM["dv"], DV_VAR(g - 2))
            for c in range(NCH):
                t.matmul(s1p[g % 2][:, :], ones_col_bf, xbg[g % 3][:, c * GT:(c + 1) * GT],
                         start=(c == 0), stop=(c == NCH - 1))
                t.wait_ge(SEM["sc"], SC_SQ(g, c))
                inst = t.matmul(s2p[g % 2][:, :], ones_col_bf, xsq[c % 2][:, :],
                                start=(c == 0), stop=(c == NCH - 1))
                inst.then_inc(SEM["pe"], 1)
                assert sm.bump("pe") == PE_ST(g, c)

        def emit_norm(t, g):
            t.wait_ge(SEM["sc"], SC_RSTD(g))
            if g == 0:
                t.wait_ge(SEM["dv"], DV_MCOL)
            if g > 0:
                t.wait_ge(SEM["dv"], DV_RC(g - 1))
            inst = t.matmul(pr[:, :], onesf[:, :], s2row[:, :], start=True, stop=True)
            inst.then_inc(SEM["pe"], 1)
            assert sm.bump("pe") == PE_BC(g)
            for c in range(NCH):
                u = g * NCH + c
                if u >= 3:
                    pg, pc = divmod(u - 3, NCH)
                    t.wait_ge(SEM["dv"], DV_MULT(pg, pc))
                t.matmul(pn[u % 3][:, :], onesb[:, :], negmb[g % 2][:, :],
                         start=True, stop=False)
                inst = t.matmul(pn[u % 3][:, :], ident[:, :],
                                xbg[g % 3][:, c * GT:(c + 1) * GT], start=False, stop=True)
                inst.then_inc(SEM["pe"], 1)
                assert sm.bump("pe") == PE_NORM(g, c)

        @blk.tensor
        def _(t):
            for g in range(NG):
                emit_stats(t, g)
                if g >= 1:
                    emit_norm(t, g - 1)
            emit_norm(t, NG - 1)

        @blk.vector
        def _(v):
            for g in range(NG):
                v.wait_ge(SEM["pe"], PE_ST(g, NCH - 1))
                fl(v, "dvf", v.tensor_copy(s1row[:, :], s1p[g % 2][:, :]))
                fl(v, "dvf", v.tensor_copy(s2row[:, :], s2p[g % 2][:, :]))
                fl(v, "dvf", v.tensor_scalar_mul(s1row[:, :], s1row[:, :], -1.0 / H))
                fl(v, "dvf", v.tensor_copy(negmb[g % 2][:, :], s1row[:, :]))
                fl(v, "dvf", v.tensor_scalar_mul(s2row[:, :], s2row[:, :], 1.0 / H))
                fl(v, "dvf", v.tensor_mul(m2row[:, :], s1row[:, :], s1row[:, :]))
                fl(v, "dvf", v.tensor_sub(s2row[:, :], s2row[:, :], m2row[:, :]))
                v.tensor_scalar_add(s2row[:, :], s2row[:, :], EPS).then_inc(SEM["dv"], 1)
                assert sm.bump("dv") == DV_VAR(g)
                v.wait_ge(SEM["pe"], PE_BC(g))
                v.tensor_copy(r128[g % 2][:, :], pr[:, :]).then_inc(SEM["dv"], 1)
                assert sm.bump("dv") == DV_RC(g)
                for c in range(NCH):
                    gc = g * NCH + c
                    sl8 = c // 8
                    if g * 4 + sl8 >= 2:
                        pg4, psl = divmod(g * 4 + sl8 - 2, 4)
                        v.wait_ge(SEM["qld"], QLD_ST8(pg4, psl))
                    v.wait_ge(SEM["pe"], PE_NORM(g, c))
                    v.tensor_tensor(xnb[(g * 4 + sl8) % 2][:, (c % 8) * GT:(c % 8 + 1) * GT],
                                    pn[gc % 3][:, :], r128[g % 2][:, :],
                                    ALU.mult).then_inc(SEM["dv"], 1)
                    assert sm.bump("dv") == DV_MULT(g, c)

        @blk.sync
        def _(sy):
            for g in range(NG):
                p, gI = gsl(g)
                for sl in range(4):
                    sy.wait_ge(SEM["dv"], DV_MULT(g, sl * 8 + 7))
                    t = dma(sy, "qld",
                            xns[p, :, gI, sl * 8:(sl + 1) * 8, :],
                            xnb[(g * 4 + sl) % 2].ap().rearrange("p (c t) -> p c t", t=GT))
                    assert t == QLD_ST8(g, sl)

    WSB = ptop("WSB", [128, 32768], BF16)

    b_qw, b_qld, b_pe, b_sc, b_qst = (sm.val(n) for n in ("qw", "qld", "pe", "sc", "qst"))
    NTILE = 8
    QW_WV = b_qw + 32
    PRECAST_DONE = QW_WV + 16 * 17
    QLD_Q4 = lambda T, q: b_qld + 16 * (T * 4 + q + 1)
    FLA1 = b_qld + 16
    PE_X = lambda T, c: b_pe + T * NCH + c + 1
    SC_E = lambda T, oc: b_sc + T * 8 + oc + 1
    QST_S = lambda T, oc: b_qst + 16 * (T * 8 + oc + 1)
    A1_KS_DONE = b_qst + 16 * 64

    def q4wait(T, c):
        q = min((c + 4) // 8, 3)
        if T == 0:
            q = max(q, 1)
        return QLD_Q4(T, q)

    with contextlib.ExitStack() as ph, nc.Block(no_gpsimd_drain=True) as blk:
        E = ph.enter_context
        wsb = WSA
        xq = [E(nc.sbuf_tensor(f"xqa{i}", [128, 8 * TT], BF16)) for i in range(4)]
        ksb = [E(nc.sbuf_tensor(f"ksb{i}", [128, 512], BF16)) for i in range(2)]
        pk = [E(nc.psum_tensor(f"pk{i}", [128, 512], F32)) for i in range(8)]

        @blk.gpsimd
        def _(g):
            for h in range(2):
                dma(g, "qw",
                    WSB.ap().rearrange("p (c o) -> p c o", o=D)[:, h * 16:(h + 1) * 16, :],
                    wv[:, h * 16:(h + 1) * 16, :])
            assert sm.val("qw") == QW_WV
            for j in range(8):
                dma(g, "qw", wfbf[j], wf[j])
                dma(g, "qw", wobf[j], wo[j])
            dma(g, "qw", bcbf, bcat)
            assert sm.val("qw") == PRECAST_DONE

        @blk.sync
        def _(sy):
            for T in range(NTILE):
                p, gI = T // 4, T % 4
                for q in range(4):
                    u = T * 4 + q
                    if u >= 4:
                        pu = u - 4
                        sy.wait_ge(SEM["pe"], PE_X(*divmod(pu * 8 + 7, NCH)))
                    t = dma(sy, "qld", xq[u % 4].ap().rearrange("p (c t) -> p c t", t=TT),
                            xns[p, :, gI, q * 8:(q + 1) * 8, :])
                    assert t == QLD_Q4(T, q)

        @blk.tensor
        def _(t):
            t.wait_ge(SEM["qw"], QW_WK)
            for T in range(NTILE):
                for c in range(NCH):
                    t.wait_ge(SEM["qld"], q4wait(T, c))
                    if T > 0 and c == 0:
                        t.wait_ge(SEM["sc"], SC_E(T - 1, 7))
                    u4 = (T * 4 + c // 8) % 4
                    for oc in range(8):
                        inst = t.matmul(pk[oc][:, :],
                                        wsb[:, c * D + oc * 128: c * D + (oc + 1) * 128],
                                        xq[u4][:, (c % 8) * TT:(c % 8 + 1) * TT],
                                        start=(c == 0), stop=(c == NCH - 1))
                    inst.then_inc(SEM["pe"], 1)
                    assert sm.bump("pe") == PE_X(T, c)

        @blk.scalar
        def _(s):
            for T in range(NTILE):
                s.wait_ge(SEM["pe"], PE_X(T, NCH - 1))
                for oc in range(8):
                    k = T * 8 + oc
                    if k >= 2:
                        pT, poc = divmod(k - 2, 8)
                        s.wait_ge(SEM["qst"], QST_S(pT, poc))
                    s.activation(ksb[k % 2][:, :], pk[oc][:, :], AF.Identity,
                                 bias=bkc[:, oc:oc + 1]).then_inc(SEM["sc"], 1)
                    assert sm.bump("sc") == SC_E(T, oc)
                    s.wait_ge(SEM["sc"], SC_E(T, oc))
                    t2 = dma(s, "qst", ks[:, oc, T * TT:(T + 1) * TT], ksb[k % 2][:, :])
                    assert t2 == QST_S(T, oc)
            s.wait_ge(SEM["qst"], A1_KS_DONE)

    acsb = ptop("ac_sb", [128, NCH * 48], BF16)

    b_qw, b_qld, b_pe, b_dv, b_qst = (sm.val(n) for n in ("qw", "qld", "pe", "dv", "qst"))
    QW_BT = b_qw + 16 * 3
    QW_WQ = b_qw + 16 * 6
    PE_BV = b_pe + 1
    DV_BV = b_dv + 1
    DV_BTOT = b_dv + 2
    QLD_Q4 = lambda T, q: b_qld + 16 * (T * 4 + q + 1)
    PE_X = lambda T, c: PE_BV + T * NCH + c + 1
    DV_E = lambda T, sub: DV_BTOT + T * 4 + sub + 1
    QST_S = lambda T, sub: b_qst + 16 * (T * 4 + sub + 1)
    A2_VS_DONE = b_qst + 16 * 32

    def q4wait2(T, c):
        q = min((c + 4) // 8, 3)
        if T == 0:
            q = max(q, 1)
        return QLD_Q4(T, q)

    with contextlib.ExitStack() as ph, nc.Block(no_gpsimd_drain=True) as blk:
        E = ph.enter_context
        wsb = WSB
        xq = [E(nc.sbuf_tensor(f"xqb{i}", [128, 8 * TT], BF16)) for i in range(4)]
        vsb = [E(nc.sbuf_tensor(f"vsb{i}", [128, D], BF16)) for i in range(2)]
        bvrow = E(nc.sbuf_tensor("bvrow", [1, D], F32))
        bv128 = E(nc.sbuf_tensor("bv128", [128, D], F32))
        borow = E(nc.sbuf_tensor("borow", [1, H], BF16))
        bfrow = E(nc.sbuf_tensor("bfrow", [1, H], BF16))
        pv = [E(nc.psum_tensor(f"pv{i}", [128, 512], F32)) for i in range(8)]

        @blk.gpsimd
        def _(g):
            dma(g, "qw", bvrow[:, :], bv_d)
            dma(g, "qw", borow[:, :], bo_d)
            dma(g, "qw", bfrow[:, :], bf_d)
            assert sm.val("qw") == QW_BT
            for h in range(2):
                dma(g, "qw",
                    WSA.ap().rearrange("p (c o) -> p c o", o=D)[:, h * 16:(h + 1) * 16, :],
                    wq[:, h * 16:(h + 1) * 16, :])
            dma(g, "qw", acsb.ap().rearrange("p (c r) -> p c r", r=48), acat)
            assert sm.val("qw") == QW_WQ

        @blk.tensor
        def _(t):
            t.wait_ge(SEM["qw"], b_qw + 16)
            for h in range(2):
                inst = t.matmul(pv[h][:, :], onesf[:, :], bvrow[:, h * 512:(h + 1) * 512],
                                start=True, stop=True)
            inst.then_inc(SEM["pe"], 1)
            assert sm.bump("pe") == PE_BV

        @blk.vector
        def _(v):
            v.wait_ge(SEM["pe"], PE_BV)
            fl(v, "dvf", v.tensor_copy(bv128[:, 0:512], pv[0][:, :]))
            v.tensor_copy(bv128[:, 512:1024], pv[1][:, :]).then_inc(SEM["dv"], 1)
            assert sm.bump("dv") == DV_BV
            v.wait_ge(SEM["qw"], QW_BT)
            v.tensor_tensor(btot[:, :], borow[:, :], bfrow[:, :], ALU.add).then_inc(SEM["dv"], 1)
            assert sm.bump("dv") == DV_BTOT

        @blk.sync
        def _(sy):
            for T in range(NTILE):
                p, gI = T // 4, T % 4
                for q in range(4):
                    u = T * 4 + q
                    if u >= 4:
                        pu = u - 4
                        sy.wait_ge(SEM["pe"], PE_X(*divmod(pu * 8 + 7, NCH)))
                    t = dma(sy, "qld", xq[u % 4].ap().rearrange("p (c t) -> p c t", t=TT),
                            xns[p, :, gI, q * 8:(q + 1) * 8, :])
                    assert t == QLD_Q4(T, q)

        @blk.tensor
        def _(t):
            t.wait_ge(SEM["dv"], DV_BV)
            t.wait_ge(SEM["qw"], QW_WV)
            for T in range(NTILE):
                for c in range(NCH):
                    t.wait_ge(SEM["qld"], q4wait2(T, c))
                    if T > 0 and c == 0:
                        t.wait_ge(SEM["dv"], DV_E(T - 1, 3))
                    u4 = (T * 4 + c // 8) % 4
                    for sub in range(4):
                        for h in range(2):
                            inst = t.matmul(pv[sub * 2 + h][:, :],
                                            xq[u4][:, (c % 8) * TT + sub * 128:
                                                   (c % 8) * TT + (sub + 1) * 128],
                                            wsb[:, c * D + h * 512: c * D + (h + 1) * 512],
                                            start=(c == 0), stop=(c == NCH - 1))
                    inst.then_inc(SEM["pe"], 1)
                    assert sm.bump("pe") == PE_X(T, c)

        @blk.vector
        def _(v):
            for T in range(NTILE):
                v.wait_ge(SEM["pe"], PE_X(T, NCH - 1))
                for sub in range(4):
                    k = T * 4 + sub
                    if k >= 2:
                        pT, psub = divmod(k - 2, 4)
                        v.wait_ge(SEM["qst"], QST_S(pT, psub))
                    fl(v, "dvf", v.tensor_tensor(vsb[k % 2][:, 0:512], pv[sub * 2][:, :],
                                                 bv128[:, 0:512], ALU.add))
                    v.tensor_tensor(vsb[k % 2][:, 512:1024], pv[sub * 2 + 1][:, :],
                                    bv128[:, 512:1024], ALU.add).then_inc(SEM["dv"], 1)
                    assert sm.bump("dv") == DV_E(T, sub)

        @blk.scalar
        def _(s):
            for T in range(NTILE):
                for sub in range(4):
                    k = T * 4 + sub
                    s.wait_ge(SEM["dv"], DV_E(T, sub))
                    t2 = dma(s, "qst", vs[:, k, :], vsb[k % 2][:, :])
                    assert t2 == QST_S(T, sub)
            s.wait_ge(SEM["qst"], A2_VS_DONE)

    b_qld, b_pe, b_dv, b_sc, b_qst = (sm.val(n) for n in ("qld", "pe", "dv", "sc", "qst"))
    NT3 = 8
    T3 = 256
    QLD_X = lambda T, q: b_qld + 16 * (T * 4 + q + 1) + (16 if T >= 1 else 0) + (32 if T >= 2 else 0)
    FLT0 = b_qld + 16 * 5
    QLD_K = b_qld + 16 * 10
    FLK = b_qld + 16 * 11
    PE_W = lambda T, w: b_pe + T * 2 + w + 1
    SC_Q = lambda T, oc: b_sc + T * 9 + oc + 1
    SC_R = lambda T: b_sc + T * 9 + 9
    QST_Q = lambda T, oc: b_qst + 16 * (T * 8 + oc + 1)
    A3_QS_DONE = b_qst + 16 * 64

    with contextlib.ExitStack() as ph, nc.Block(no_gpsimd_drain=True) as blk:
        E = ph.enter_context
        wsb = WSA
        xnt = [E(nc.sbuf_tensor(f"xnt{i}", [128, NCH * T3], BF16)) for i in range(2)]
        qsb = [E(nc.sbuf_tensor(f"qsb{i}", [128, T3], BF16)) for i in range(2)]
        pq = [E(nc.psum_tensor(f"pq{i}", [128, T3], F32)) for i in range(4)]
        prr = E(nc.psum_tensor("prr", [48, T3], F32))

        @blk.sync
        def _(sy):
            for T in range(NT3):
                if T >= 2:
                    sy.wait_ge(SEM["pe"], PE_W(T - 2, 1))
                gI, half = T // 2, T % 2
                for q in range(4):
                    t = dma(sy, "qld",
                            xnt[T % 2].ap().rearrange("p (c t) -> p c t", t=T3)[:, q * 8:(q + 1) * 8, :],
                            xns[0, :, gI, q * 8:(q + 1) * 8, half * T3:(half + 1) * T3])
                    assert t == QLD_X(T, q)
                if T == 0:
                    assert dma(sy, "qld", flsh, xnt[0][:, 0:2]) == FLT0
                if T == 1:
                    sy.wait_ge(SEM["qst"], A1_KS_DONE)
                    t = dma(sy, "qld", WSB.ap().rearrange("p (a t) -> p a t", t=S), ks)
                    assert t == QLD_K
                    assert dma(sy, "qld", flsh, WSB[:, 0:2]) == FLK

        @blk.tensor
        def _(t):
            t.wait_ge(SEM["qw"], QW_WQ)
            for T in range(NT3):
                t.wait_ge(SEM["qld"], FLT0 if T == 0 else QLD_X(T, 3))
                for w in range(2):
                    for c in range(NCH):
                        for i in range(4):
                            oc = w * 4 + i
                            if c == 0 and (T > 0 or w > 0):
                                pT, pw = (T, w - 1) if w > 0 else (T - 1, 1)
                                t.wait_ge(SEM["sc"], SC_Q(pT, pw * 4 + i))
                            inst = t.matmul(pq[i][:, :],
                                            wsb[:, c * D + oc * 128: c * D + (oc + 1) * 128],
                                            xnt[T % 2][:, c * T3:(c + 1) * T3],
                                            start=(c == 0), stop=(c == NCH - 1))
                        if w == 1:
                            if c == 0 and T > 0:
                                t.wait_ge(SEM["sc"], SC_R(T - 1))
                            inst = t.matmul(prr[:, :], acsb[:, c * 48:(c + 1) * 48],
                                            xnt[T % 2][:, c * T3:(c + 1) * T3],
                                            start=(c == 0), stop=(c == NCH - 1))
                    inst.then_inc(SEM["pe"], 1)
                    assert sm.bump("pe") == PE_W(T, w)

        @blk.scalar
        def _(s):
            for T in range(NT3):
                for w in range(2):
                    s.wait_ge(SEM["pe"], PE_W(T, w))
                    for i in range(4):
                        oc = w * 4 + i
                        k = T * 8 + oc
                        if k >= 2:
                            pT, poc = divmod(k - 2, 8)
                            s.wait_ge(SEM["qst"], QST_Q(pT, poc))
                        s.activation(qsb[k % 2][:, :], pq[i][:, :], AF.Identity,
                                     bias=bq32[:, oc:oc + 1],
                                     scale=1.0 / 32.0).then_inc(SEM["sc"], 1)
                        assert sm.bump("sc") == SC_Q(T, oc)
                        s.wait_ge(SEM["sc"], SC_Q(T, oc))
                        t2 = dma(s, "qst",
                                 qs[T * 2:(T + 1) * 2, :, oc, :].rearrange("q p t -> p q t"),
                                 qsb[k % 2].ap().rearrange("p (q t) -> p q t", t=128))
                        assert t2 == QST_Q(T, oc)
                    if w == 1:
                        s.wait_ge(SEM["dv"], DV_MCOL)
                        s.activation(rsb[:, T * T3:(T + 1) * T3], prr[:, :], AF.Identity,
                                     scale=mcol[:, :]).then_inc(SEM["sc"], 1)
                        assert sm.bump("sc") == SC_R(T)
            s.wait_ge(SEM["qst"], A3_QS_DONE)

    b_qld, b_pe, b_dv, b_sc, b_qst = (sm.val(n) for n in ("qld", "pe", "dv", "sc", "qst"))
    NQT = 16
    QLD_V = b_qld + 16
    FLV = b_qld + 32
    QLD_Q = lambda qt: b_qld + 32 + 16 * (qt + 1) + (16 if qt >= 1 else 0)
    FLQ = b_qld + 32 + 32
    PE_S = lambda qt, kb: b_pe + qt * 19 + kb + 1
    PE_PTG = lambda qt, g: b_pe + qt * 19 + 8 + g + 1
    PE_PV = lambda qt: b_pe + qt * 19 + 17
    PE_OTG = lambda qt, g2: b_pe + qt * 19 + 17 + g2 + 1
    DV_NMX = lambda qt: b_dv + qt * 13 + 1
    DV_RL = lambda qt: b_dv + qt * 13 + 2
    DV_PTC = lambda qt, g: b_dv + qt * 13 + 2 + g + 1
    DV_OEV = lambda qt: b_dv + qt * 13 + 11
    DV_OTC = lambda qt, g2: b_dv + qt * 13 + 11 + g2 + 1
    SC_SE = lambda qt, kb: b_sc + qt * 9 + kb + 1
    SC_EXP = lambda qt: b_sc + qt * 9 + 9
    QST_OT = lambda qt: b_qst + 16 * (qt + 1)
    B_OT_DONE = b_qst + 16 * 16

    def DV_PT_USE(u):
        q, r = divmod(u, 10)
        return DV_PTC(q, r) if r < 8 else DV_OTC(q, r - 8)

    with contextlib.ExitStack() as ph, nc.Block(no_gpsimd_drain=True) as blk:
        E = ph.enter_context
        k_sb = WSB
        v_sb = WSA
        qtb = [E(nc.sbuf_tensor(f"qtb{i}", [128, 8 * 128], BF16)) for i in range(2)]
        s_sb = E(nc.sbuf_tensor("s_sb", [128, S], F32))
        p_sb = E(nc.sbuf_tensor("p_sb", [128, S], BF16))
        pt_sb = E(nc.sbuf_tensor("pt_sb", [128, NCH * 128], BF16))
        osb = E(nc.sbuf_tensor("osb", [128, D], BF16))
        otb = [E(nc.sbuf_tensor(f"otb{i}", [128, D], BF16)) for i in range(2)]
        nmx = E(nc.sbuf_tensor("nmx", [128, 1], F32))
        lsum = E(nc.sbuf_tensor("lsum", [128, 1], F32))
        rl = E(nc.sbuf_tensor("rl", [128, 1], F32))
        ps = [E(nc.psum_tensor(f"ps{i}", [128, 512], F32)) for i in range(2)]
        pt = [E(nc.psum_tensor(f"pt{i}", [128, 512], BF16)) for i in range(2)]
        po = [E(nc.psum_tensor(f"po{i}", [128, 512], F32)) for i in range(2)]

        @blk.sync
        def _(sy):
            sy.wait_ge(SEM["qst"], A2_VS_DONE)
            t = dma(sy, "qld", v_sb.ap().rearrange("p (a t) -> p a t", t=D), vs)
            assert t == QLD_V
            assert dma(sy, "qld", flsh, v_sb[:, 0:2]) == FLV
            sy.wait_ge(SEM["qst"], A3_QS_DONE)
            for qt in range(NQT):
                if qt >= 2:
                    sy.wait_ge(SEM["pe"], PE_S(qt - 2, 7))
                t = dma(sy, "qld",
                        qtb[qt % 2].ap().rearrange("p (a t) -> p a t", t=128), qs[qt])
                if qt == 0:
                    assert dma(sy, "qld", flsh, qtb[0][:, 0:2]) == FLQ
                else:
                    assert t == QLD_Q(qt)

        @blk.tensor
        def _(t):
            t.wait_ge(SEM["qld"], FLK)
            ptu = 0
            for qt in range(NQT):
                t.wait_ge(SEM["qld"], FLQ if qt == 0 else QLD_Q(qt))
                for kb in range(8):
                    k = qt * 8 + kb
                    if k >= 2:
                        pqt, pkb = divmod(k - 2, 8)
                        t.wait_ge(SEM["sc"], SC_SE(pqt, pkb))
                    for oc in range(8):
                        inst = t.matmul(ps[kb % 2][:, :],
                                        qtb[qt % 2][:, oc * 128:(oc + 1) * 128],
                                        k_sb[:, oc * S + kb * 512: oc * S + (kb + 1) * 512],
                                        start=(oc == 0), stop=(oc == 7))
                    inst.then_inc(SEM["pe"], 1)
                    assert sm.bump("pe") == PE_S(qt, kb)
                t.wait_ge(SEM["sc"], SC_EXP(qt))
                for g in range(8):
                    if ptu >= 2:
                        t.wait_ge(SEM["dv"], DV_PT_USE(ptu - 2))
                    for i in range(4):
                        kc = g * 4 + i
                        inst = t.transpose(pt[ptu % 2][:, i * 128:(i + 1) * 128],
                                           p_sb[:, kc * 128:(kc + 1) * 128], ident[:, :])
                    inst.then_inc(SEM["pe"], 1)
                    assert sm.bump("pe") == PE_PTG(qt, g)
                    ptu += 1
                if qt == 0:
                    t.wait_ge(SEM["qld"], FLV)
                else:
                    t.wait_ge(SEM["dv"], DV_OEV(qt - 1))
                for kc in range(NCH):
                    t.wait_ge(SEM["dv"], DV_PTC(qt, kc // 4))
                    for h in range(2):
                        inst = t.matmul(po[h][:, :],
                                        pt_sb[:, kc * 128:(kc + 1) * 128],
                                        v_sb[:, kc * D + h * 512: kc * D + (h + 1) * 512],
                                        start=(kc == 0), stop=(kc == NCH - 1))
                inst.then_inc(SEM["pe"], 1)
                assert sm.bump("pe") == PE_PV(qt)
                t.wait_ge(SEM["dv"], DV_OEV(qt))
                for g2 in range(2):
                    if ptu >= 2:
                        t.wait_ge(SEM["dv"], DV_PT_USE(ptu - 2))
                    for i in range(4):
                        dc = g2 * 4 + i
                        inst = t.transpose(pt[ptu % 2][:, i * 128:(i + 1) * 128],
                                           osb[:, dc * 128:(dc + 1) * 128], ident[:, :])
                    inst.then_inc(SEM["pe"], 1)
                    assert sm.bump("pe") == PE_OTG(qt, g2)
                    ptu += 1

        @blk.scalar
        def _(s):
            for qt in range(NQT):
                for kb in range(8):
                    s.wait_ge(SEM["pe"], PE_S(qt, kb))
                    s.activation(s_sb[:, kb * 512:(kb + 1) * 512], ps[kb % 2][:, :],
                                 AF.Copy).then_inc(SEM["sc"], 1)
                    assert sm.bump("sc") == SC_SE(qt, kb)
                s.wait_ge(SEM["dv"], DV_NMX(qt))
                s.activation(p_sb[:, :], s_sb[:, :], AF.Exp, bias=nmx[:, :],
                             accum_out=lsum[:, :]).then_inc(SEM["sc"], 1)
                assert sm.bump("sc") == SC_EXP(qt)
                s.wait_ge(SEM["sc"], SC_EXP(qt))

        @blk.vector
        def _(v):
            ptu = 0
            for qt in range(NQT):
                v.wait_ge(SEM["sc"], SC_SE(qt, 7))
                v.reduce_max(nmx[:, :], s_sb[:, :], axis=AX.X, negate=True).then_inc(SEM["dv"], 1)
                assert sm.bump("dv") == DV_NMX(qt)
                v.wait_ge(SEM["sc"], SC_EXP(qt))
                v.reciprocal(rl[:, :], lsum[:, :]).then_inc(SEM["dv"], 1)
                assert sm.bump("dv") == DV_RL(qt)
                for g in range(8):
                    v.wait_ge(SEM["pe"], PE_PTG(qt, g))
                    v.tensor_copy(pt_sb[:, g * 512:(g + 1) * 512],
                                  pt[ptu % 2][:, :]).then_inc(SEM["dv"], 1)
                    assert sm.bump("dv") == DV_PTC(qt, g)
                    ptu += 1
                v.wait_ge(SEM["pe"], PE_PV(qt))
                if qt > 0:
                    v.wait_ge(SEM["pe"], PE_OTG(qt - 1, 1))
                fl(v, "dvf", v.tensor_scalar_mul(osb[:, 0:512], po[0][:, :], rl[:, :]))
                v.tensor_scalar_mul(osb[:, 512:1024], po[1][:, :], rl[:, :]).then_inc(SEM["dv"], 1)
                assert sm.bump("dv") == DV_OEV(qt)
                for g2 in range(2):
                    v.wait_ge(SEM["pe"], PE_OTG(qt, g2))
                    if qt >= 2 and g2 == 0:
                        v.wait_ge(SEM["qst"], QST_OT(qt - 2))
                    v.tensor_copy(otb[qt % 2][:, g2 * 512:(g2 + 1) * 512],
                                  pt[ptu % 2][:, :]).then_inc(SEM["dv"], 1)
                    assert sm.bump("dv") == DV_OTC(qt, g2)
                    ptu += 1

        @blk.gpsimd
        def _(g):
            for qt in range(NQT):
                g.wait_ge(SEM["dv"], DV_OTC(qt, 1))
                t = dma(g, "qst", ots[qt].rearrange("p a t -> p (a t)"), otb[qt % 2][:, :])
                assert t == QST_OT(qt)

    b_qw, b_qld, b_pe, b_dv, b_qst = (sm.val(n) for n in ("qw", "qld", "pe", "dv", "qst"))
    NJ = 8
    QLD_G = lambda j, gI: b_qld + 16 * (j * 4 + gI + 1)
    QW_T = lambda j, ti: b_qw + 16 * (j * 32 + (ti + 1) * 2)
    PE_T = lambda j, ti: b_pe + j * 16 + ti + 1
    DV_T = lambda j, ti: b_dv + j * 16 + ti + 1

    def qst_at(j, ti):
        n = 8 + j * 20 + ti + 1 if j <= NJ - 2 else 8 + (NJ - 2) * 20 + (j - (NJ - 2)) * 16 + ti + 1
        return b_qst + 16 * n

    def qw_wj(j):
        if j < 2:
            return b_qst + 16 * 4 * (j + 1)
        return b_qst + 16 * (8 + (j - 2) * 20 + 20)

    with contextlib.ExitStack() as ph, nc.Block(no_gpsimd_drain=True) as blk:
        E = ph.enter_context
        wfsb = [WSA[:, 0:16384], WSA[:, 16384:32768]]
        xng2 = E(nc.sbuf_tensor("xng2", [128, NCH * 512], BF16))
        xng = [WSB[:, 0:16384], WSB[:, 16384:32768], xng2.ap()]
        wosb = [E(nc.sbuf_tensor(f"wo_sb{i}", [128, 8 * 512], BF16)) for i in range(2)]
        bcsb = [E(nc.sbuf_tensor(f"bc_sb{i}", [48, 512], BF16)) for i in range(2)]
        otc = [E(nc.sbuf_tensor(f"otc{i}", [128, 8 * 128], BF16)) for i in range(2)]
        xrs = [E(nc.sbuf_tensor(f"xrs{i}", [128, 512], F32)) for i in range(2)]
        outb = [E(nc.sbuf_tensor(f"outb{i}", [128, 512], F32)) for i in range(2)]
        pc = [E(nc.psum_tensor(f"pc{i}", [128, 512], F32)) for i in range(2)]

        def wload(s, j):
            dma(s, "qst", wfsb[j % 2].rearrange("p (c o) -> p c o", o=512), wfbf[j])
            dma(s, "qst", wosb[j % 2].ap().rearrange("p (c o) -> p c o", o=512), wobf[j])
            dma(s, "qst", bcsb[j % 2][:, :], bcbf[j])
            return dma(s, "qst", flsh, wfsb[j % 2][:, 0:2])

        @blk.scalar
        def _(s):
            s.wait_ge(SEM["qw"], PRECAST_DONE)
            for j in range(2):
                assert wload(s, j) == qw_wj(j)
            for j in range(NJ):
                for ti in range(16):
                    k = j * 16 + ti
                    s.wait_ge(SEM["dv"], DV_T(j, ti))
                    t2 = dma(s, "qst",
                             out_d[ti * 128:(ti + 1) * 128, j * 512:(j + 1) * 512],
                             outb[k % 2][:, :])
                    assert t2 == qst_at(j, ti)
                if j + 2 <= NJ - 1:
                    assert wload(s, j + 2) == qw_wj(j + 2)
            s.wait_ge(SEM["qst"], sm.val("qst"))

        @blk.sync
        def _(sy):
            sy.wait_ge(SEM["qst"], B_OT_DONE)
            for j in range(NJ):
                for gI in range(4):
                    gi = j * 4 + gI
                    if gi >= 3:
                        pj, pg = divmod(gi - 3, 4)
                        sy.wait_ge(SEM["pe"], PE_T(pj, pg * 4 + 3))
                    t = dma(sy, "qld", xng[gi % 3].rearrange("p (c t) -> p c t", t=512),
                            xns[0, :, gI, :, :])
                    assert t == QLD_G(j, gI)

        @blk.gpsimd
        def _(g):
            for j in range(NJ):
                for ti in range(16):
                    k = j * 16 + ti
                    if k >= 2:
                        pj, pti = divmod(k - 2, 16)
                        g.wait_ge(SEM["dv"], DV_T(pj, pti))
                    dma(g, "qw",
                        otc[k % 2].ap().rearrange("p (a t) -> p a t", t=128), ots[ti])
                    t = dma(g, "qw", xrs[k % 2][:, :],
                            x_own[ti * 128:(ti + 1) * 128, j * 512:(j + 1) * 512])
                    assert t == QW_T(j, ti)

        @blk.tensor
        def _(t):
            t.wait_ge(SEM["dv"], DV_BTOT)
            for j in range(NJ):
                t.wait_ge(SEM["qst"], qw_wj(j))
                for gI in range(4):
                    gi = j * 4 + gI
                    t.wait_ge(SEM["qld"], QLD_G(j, gI))
                    for tt in range(4):
                        ti = gI * 4 + tt
                        k = j * 16 + ti
                        t.wait_ge(SEM["qw"], QW_T(j, min(ti + 1, 15)))
                        if k >= 2:
                            pj, pti = divmod(k - 2, 16)
                            t.wait_ge(SEM["dv"], DV_T(pj, pti))
                        for c in range(NCH):
                            t.matmul(pc[k % 2][:, :],
                                     xng[gi % 3][:, c * 512 + tt * 128: c * 512 + (tt + 1) * 128],
                                     wfsb[j % 2][:, c * 512:(c + 1) * 512],
                                     start=(c == 0), stop=False)
                        for dc in range(8):
                            t.matmul(pc[k % 2][:, :],
                                     otc[k % 2][:, dc * 128:(dc + 1) * 128],
                                     wosb[j % 2][:, dc * 512:(dc + 1) * 512],
                                     start=False, stop=False)
                        t.matmul(pc[k % 2][:, :], rsb[:, ti * 128:(ti + 1) * 128],
                                 bcsb[j % 2][:, :], start=False, stop=False)
                        inst = t.matmul(pc[k % 2][:, :], onesb[:, :],
                                        btot[:, j * 512:(j + 1) * 512], start=False, stop=True)
                        inst.then_inc(SEM["pe"], 1)
                        assert sm.bump("pe") == PE_T(j, ti)

        @blk.vector
        def _(v):
            for j in range(NJ):
                for ti in range(16):
                    k = j * 16 + ti
                    v.wait_ge(SEM["pe"], PE_T(j, ti))
                    if k >= 2:
                        pj, pti = divmod(k - 2, 16)
                        v.wait_ge(SEM["qst"], qst_at(pj, pti))
                    v.tensor_tensor(outb[k % 2][:, :], pc[k % 2][:, :], xrs[k % 2][:, :],
                                    ALU.add).then_inc(SEM["dv"], 1)
                    assert sm.bump("dv") == DV_T(j, ti)

    top.close()
    sm.ctx.close()
    return nc


def kernel(L0_attn_x, L0_attn_Wqkv, L0_attn_bqkv, L0_attn_Wo, L0_attn_bo,
           L0_ffn_weight, L0_ffn_bias, L0_ffn_lora_A, L0_ffn_lora_B,
           L0_ffn_lora_start_indices):
    x = np.asarray(L0_attn_x, dtype=np.float32)
    Wqkv = np.asarray(L0_attn_Wqkv, dtype=np.float32)
    bqkv = np.asarray(L0_attn_bqkv, dtype=np.float32)
    Wo = np.asarray(L0_attn_Wo, dtype=np.float32)
    bo = np.asarray(L0_attn_bo, dtype=np.float32)
    Wff = np.asarray(L0_ffn_weight, dtype=np.float32)
    bff = np.asarray(L0_ffn_bias, dtype=np.float32)
    lA = np.asarray(L0_ffn_lora_A, dtype=np.float32)
    lB = np.asarray(L0_ffn_lora_B, dtype=np.float32)
    lsi = np.asarray(L0_ffn_lora_start_indices, dtype=np.int32)

    bf = ml_dtypes.bfloat16

    def pmaj(w):
        o = w.shape[1]
        return np.ascontiguousarray(w.reshape(NCH, 128, o).transpose(1, 0, 2))

    wq_h = pmaj(Wqkv[:, 0:D])
    wk_h = pmaj(Wqkv[:, D:2 * D])
    wv_h = pmaj(Wqkv[:, 2 * D:3 * D])
    wf_h = np.ascontiguousarray(Wff.T.reshape(NCH, 128, 8, 512).transpose(2, 1, 0, 3))
    wo_h = np.ascontiguousarray(Wo.reshape(8, 128, 8, 512).transpose(2, 1, 0, 3))
    acat_h = np.ascontiguousarray(lA.reshape(48, H).T.reshape(NCH, 128, 48).transpose(1, 0, 2))
    bcat_h = np.ascontiguousarray(lB.transpose(0, 2, 1).reshape(48, 8, 512).transpose(1, 0, 2))
    bqc_h = np.ascontiguousarray(bqkv[0:D].reshape(8, 128).T)
    bkc_h = np.ascontiguousarray(bqkv[D:2 * D].reshape(8, 128).T)
    bv_h = np.ascontiguousarray(bqkv[2 * D:3 * D][None, :])
    bo_h = np.ascontiguousarray(bo[None, :])
    bf_h = np.ascontiguousarray(bff[None, :])
    lsi_h = np.ascontiguousarray(lsi[None, :])
    idxp_h = np.repeat(np.arange(3, dtype=np.float32), 16)[None, :].copy()
    onesf_h = np.ones((1, 128), dtype=np.float32)
    onesb_h = np.ones((1, 128), dtype=bf)
    ident_h = np.eye(128, dtype=bf)

    xt_half = {}
    for b in range(B):
        for h in range(2):
            xt_half[(b, h)] = np.ascontiguousarray(
                x[b, h * SOWN:(h + 1) * SOWN, :].T.reshape(NCH, 128, 4, GT).transpose(1, 2, 0, 3))

    in_maps = []
    for c in range(8):
        b, h = c // 2, c % 2
        in_maps.append(dict(
            xt0=xt_half[(b, h)], xt1=xt_half[(b, 1 - h)],
            x_own=np.ascontiguousarray(x[b, h * SOWN:(h + 1) * SOWN, :]),
            wq=wq_h, wk=wk_h, wv=wv_h, wf=wf_h, wo=wo_h,
            acat=acat_h, bcat=bcat_h,
            bqc=bqc_h, bkc=bkc_h, bv_row=bv_h, bo_row=bo_h, bf_row=bf_h,
            lsi=lsi_h, bvec=np.array([[float(b)]], dtype=np.float32),
            idxp=idxp_h, onesf=onesf_h, onesb=onesb_h, ident=ident_h,
        ))

    nc = build_nc()
    res = run_bass_kernel_spmd(nc, in_maps, core_ids=list(range(8)))

    out = np.empty((B, S, H), dtype=np.float32)
    for c in range(8):
        b, h = c // 2, c % 2
        out[b, h * SOWN:(h + 1) * SOWN, :] = res.results[c]["out"]
    return out


# version 16
# speedup vs baseline: 1.2554x; 1.2554x over previous
import contextlib

import numpy as np
import ml_dtypes

import concourse.bass as bass
import concourse.mybir as mybir
from concourse.bass_utils import run_bass_kernel_spmd

F32 = mybir.dt.float32
BF16 = mybir.dt.bfloat16
I32 = mybir.dt.int32
AF = mybir.ActivationFunctionType
ALU = mybir.AluOpType
AX = mybir.AxisListType

H = 4096
S = 4096
B = 4
D = 1024
SOWN = 2048
NCH = 32
GT = 512
NG = 8
TT = 512
NTO = 4
EPS = 1e-5


class Ctrs:
    def __init__(self, nc, names):
        self.h = {}
        self.c = {}
        self.ctx = contextlib.ExitStack()
        for n in names:
            self.h[n] = self.ctx.enter_context(nc.semaphore(n))
            self.c[n] = 0

    def bump(self, n, k=1):
        self.c[n] += k
        return self.c[n]

    def val(self, n):
        return self.c[n]


def build_nc():
    nc = bass.Bass("TRN2", target_bir_lowering=False)

    def din(name, shape, dt=F32):
        return nc.dram_tensor(name, shape, dt, kind="ExternalInput").ap()

    xt0 = din("xt0", [128, 4, NCH, GT])
    xt1 = din("xt1", [128, 4, NCH, GT])
    x_own = din("x_own", [SOWN, H])
    wq = din("wq", [128, NCH, D])
    wk = din("wk", [128, NCH, D])
    wv = din("wv", [128, NCH, D])
    wf = din("wf", [8, 128, NCH, 512])
    wo = din("wo", [8, 128, 8, 512])
    acat = din("acat", [128, NCH, 48])
    bcat = din("bcat", [8, 48, 512])
    bqc_d = din("bqc", [128, 8])
    bkc_d = din("bkc", [128, 8])
    bv_d = din("bv_row", [1, D])
    bo_d = din("bo_row", [1, H])
    bf_d = din("bf_row", [1, H])
    lsi_d = din("lsi", [1, 4], I32)
    bvec_d = din("bvec", [1, 1])
    idxp_d = din("idxp", [1, 48])
    onesf_d = din("onesf", [1, 128])
    onesb_d = din("onesb", [1, 128], BF16)
    ident_d = din("ident", [128, 128], BF16)

    out_d = nc.dram_tensor("out", [SOWN, H], F32, kind="ExternalOutput").ap()

    xns = nc.dram_tensor("xns", [2, 128, 4, NCH, GT], BF16).ap()
    ks = nc.dram_tensor("ks", [128, 8, S], BF16).ap()
    vs = nc.dram_tensor("vs", [128, NCH, D], BF16).ap()
    qs = nc.dram_tensor("qs", [16, 128, 8, 128], BF16).ap()
    ots = nc.dram_tensor("ots", [128, 16, 8, 128], BF16).ap()
    flsh = nc.dram_tensor("flsh", [128, 2], BF16).ap()

    sm = Ctrs(nc, ["qld", "qw", "qst", "pe", "dv", "sc", "dvf", "scf"])
    SEM = sm.h

    def fl(eng, name, inst):
        t = sm.bump(name)
        inst.then_inc(SEM[name], 1)
        eng.wait_ge(SEM[name], t)

    def dma(eng, name, out, in_):
        t = sm.bump(name, 16)
        eng.dma_start(out=out, in_=in_).then_inc(SEM[name], 16)
        return t

    top = contextlib.ExitStack()

    def ptop(name, shape, dt):
        return top.enter_context(nc.sbuf_tensor(name, shape, dt, side="right"))

    onesf = ptop("onesf_s", [1, 128], F32)
    onesb = ptop("onesb_s", [1, 128], BF16)
    ident = ptop("ident_s", [128, 128], BF16)
    bqc = ptop("bqc_s", [128, 8], F32)
    bkc = ptop("bkc_s", [128, 8], F32)
    bq32 = ptop("bq32", [128, 8], F32)
    lsii = ptop("lsii_s", [1, 4], I32)
    lsif = ptop("lsif", [1, 4], F32)
    bvec = ptop("bvec_s", [1, 1], F32)
    idxp = ptop("idxp_s", [1, 48], F32)
    ge = ptop("ge", [1, 4], F32)
    sg = ptop("sg", [1, 8], F32)
    mrow = ptop("mrow", [1, 48], F32)
    mcol = ptop("mcol", [48, 1], F32)
    rsb = ptop("rsb", [48, SOWN], BF16)
    btot = ptop("btot", [1, H], BF16)
    ones_col_bf = nc.const_aps.tensor(1.0, [128, 1], BF16)

    WSA = ptop("WSA", [128, 32768], BF16)

    b_qw, b_qld, b_pe, b_dv, b_sc = (sm.val(n) for n in ("qw", "qld", "pe", "dv", "sc"))
    QW_TINY = b_qw + 16 * 8
    QW_G = lambda g: QW_TINY + 16 * (g + 1) + (16 if g >= 1 else 0) + (32 if g >= 4 else 0)
    FLG0 = QW_TINY + 32
    QW_WK = QW_TINY + 16 * 6 + 16
    DV_MROW = b_dv + 1
    DV_MCOL = b_dv + 2
    PE_MASK = b_pe + 1
    PE_ST = lambda g, c: PE_MASK + 32 * g + max(g - 1, 0) + c + 1
    PE_BC = lambda g: PE_MASK + 32 * (min(g + 1, NG - 1) + 1) + g + 1
    SC_SQ = lambda g, c: b_sc + g * 33 + c + 1
    SC_RSTD = lambda g: b_sc + g * 33 + 33
    DV_VAR = lambda g: DV_MCOL + g * 35 + 1
    DV_NM = lambda g: DV_MCOL + g * 35 + 2
    DV_RC = lambda g: DV_MCOL + g * 35 + 3
    DV_MULT = lambda g, c: DV_MCOL + g * 35 + 3 + c + 1
    QLD_ST8 = lambda g, sl: b_qld + 16 * (g * 8 + sl + 1)

    with contextlib.ExitStack() as ph, nc.Block(no_gpsimd_drain=True) as blk:
        E = ph.enter_context
        xbg = [E(nc.sbuf_tensor(f"xbg{i}", [128, NCH * GT], BF16)) for i in range(3)]
        xsq = [E(nc.sbuf_tensor(f"xsq{i}", [128, GT], BF16)) for i in range(2)]
        s1row = E(nc.sbuf_tensor("s1row", [1, GT], F32))
        s2row = E(nc.sbuf_tensor("s2row", [1, GT], F32))
        m2row = E(nc.sbuf_tensor("m2row", [1, GT], F32))
        nm128 = [E(nc.sbuf_tensor(f"nm128_{i}", [128, GT], F32)) for i in range(2)]
        r128 = [E(nc.sbuf_tensor(f"r128_{i}", [128, GT], F32)) for i in range(2)]
        tmpf = [E(nc.sbuf_tensor(f"tmpf{i}", [128, GT], F32)) for i in range(4)]
        xnb = [E(nc.sbuf_tensor(f"xnb{i}", [128, 4 * GT], BF16)) for i in range(2)]
        s1p = [E(nc.psum_tensor(f"s1p{i}", [1, GT], F32)) for i in range(2)]
        s2p = [E(nc.psum_tensor(f"s2p{i}", [1, GT], F32)) for i in range(2)]
        pm = E(nc.psum_tensor("pm", [128, GT], F32))
        pr = E(nc.psum_tensor("pr", [128, GT], F32))
        pmask = pm

        xt = [xt0, xt1]

        def gsl(g):
            return g // 4, g % 4

        @blk.gpsimd
        def _(g_):
            dma(g_, "qw", onesf[:, :], onesf_d)
            dma(g_, "qw", onesb[:, :], onesb_d)
            dma(g_, "qw", ident[:, :], ident_d)
            dma(g_, "qw", bqc[:, :], bqc_d)
            dma(g_, "qw", bkc[:, :], bkc_d)
            dma(g_, "qw", lsii[:, :], lsi_d)
            dma(g_, "qw", bvec[:, :], bvec_d)
            dma(g_, "qw", idxp[:, :], idxp_d)
            assert sm.val("qw") == QW_TINY
            for g in range(NG):
                p, gI = gsl(g)
                if g >= 3:
                    g_.wait_ge(SEM["dv"], DV_MULT(g - 3, NCH - 1))
                t = dma(g_, "qw",
                        xbg[g % 3].ap().rearrange("p (c t) -> p c t", t=GT),
                        xt[p][:, gI, :, :])
                if g == 0:
                    assert dma(g_, "qw", flsh, xbg[0][:, 0:2]) == FLG0
                else:
                    assert t == QW_G(g)
                if g == 3:
                    for h in range(2):
                        dma(g_, "qw",
                            WSA.ap().rearrange("p (c o) -> p c o", o=D)[:, h * 16:(h + 1) * 16, :],
                            wk[:, h * 16:(h + 1) * 16, :])
                    assert sm.val("qw") == QW_WK

        @blk.vector
        def _(v):
            v.wait_ge(SEM["qw"], QW_TINY)
            fl(v, "dvf", v.tensor_scalar_mul(bq32[:, :], bqc[:, :], 1.0 / 32.0))
            fl(v, "dvf", v.tensor_copy(lsif[:, :], lsii[:, :]))
            fl(v, "dvf", v.tensor_single_scalar(ge[:, :], lsif[:, :], bvec[:, :], ALU.is_le))
            fl(v, "dvf", v.tensor_scalar_add(sg[:, 0:1], ge[:, 3:4], 3.0))
            fl(v, "dvf", v.tensor_scalar(sg[:, 3:4], ge[:, 1:2], -1.0, 1.0, ALU.mult, ALU.add))
            fl(v, "dvf", v.tensor_mul(sg[:, 3:4], sg[:, 3:4], ge[:, 0:1]))
            fl(v, "dvf", v.tensor_scalar(sg[:, 1:2], ge[:, 1:2], 2.0, None, ALU.mult))
            fl(v, "dvf", v.tensor_add(sg[:, 1:2], sg[:, 1:2], sg[:, 3:4]))
            fl(v, "dvf", v.tensor_mul(sg[:, 0:1], sg[:, 0:1], ge[:, 2:3]))
            fl(v, "dvf", v.tensor_scalar(sg[:, 3:4], ge[:, 2:3], -1.0, 1.0, ALU.mult, ALU.add))
            fl(v, "dvf", v.tensor_mul(sg[:, 1:2], sg[:, 1:2], sg[:, 3:4]))
            fl(v, "dvf", v.tensor_add(sg[:, 0:1], sg[:, 0:1], sg[:, 1:2]))
            fl(v, "dvf", v.tensor_scalar(sg[:, 1:2], sg[:, 0:1], -2.0, 0.0, ALU.add, ALU.max))
            fl(v, "dvf", v.tensor_scalar_min(sg[:, 1:2], sg[:, 1:2], 2.0))
            fl(v, "dvf", v.tensor_single_scalar(sg[:, 2:3], sg[:, 0:1], 1.0, ALU.is_gt))
            fl(v, "dvf", v.tensor_single_scalar(mrow[:, :], idxp[:, :], sg[:, 1:2], ALU.is_equal))
            v.tensor_single_scalar(mrow[:, :], mrow[:, :], sg[:, 2:3], ALU.mult).then_inc(SEM["dv"], 1)
            assert sm.bump("dv") == DV_MROW

        @blk.tensor
        def _(t):
            t.wait_ge(SEM["dv"], DV_MROW)
            t.matmul(pmask[0:48, 0:1], mrow[:, :], onesf[:, 0:1], start=True, stop=True).then_inc(SEM["pe"], 1)
            assert sm.bump("pe") == PE_MASK

        @blk.vector
        def _(v):
            v.wait_ge(SEM["pe"], PE_MASK)
            v.tensor_copy(mcol[:, :], pmask[0:48, 0:1]).then_inc(SEM["dv"], 1)
            assert sm.bump("dv") == DV_MCOL

        @blk.scalar
        def _(s):
            for g in range(NG):
                for c in range(NCH):
                    if c == 0:
                        s.wait_ge(SEM["qw"], FLG0 if g == 0 else QW_G(g))
                    gc = g * NCH + c
                    if gc >= 2:
                        pg, pc = divmod(gc - 2, NCH)
                        s.wait_ge(SEM["pe"], PE_ST(pg, pc))
                    s.square(xsq[c % 2][:, :], xbg[g % 3][:, c * GT:(c + 1) * GT]).then_inc(SEM["sc"], 1)
                    assert sm.bump("sc") == SC_SQ(g, c)
                s.wait_ge(SEM["dv"], DV_VAR(g))
                fl(s, "scf", s.activation(s2row[:, :], s2row[:, :], AF.Ln))
                s.activation(s2row[:, :], s2row[:, :], AF.Exp, scale=-0.5).then_inc(SEM["sc"], 1)
                assert sm.bump("sc") == SC_RSTD(g)

        def emit_stats(t, g):
            t.wait_ge(SEM["qw"], FLG0 if g == 0 else QW_G(g))
            if g >= 2:
                t.wait_ge(SEM["dv"], DV_VAR(g - 2))
            for c in range(NCH):
                t.matmul(s1p[g % 2][:, :], ones_col_bf, xbg[g % 3][:, c * GT:(c + 1) * GT],
                         start=(c == 0), stop=(c == NCH - 1))
                t.wait_ge(SEM["sc"], SC_SQ(g, c))
                inst = t.matmul(s2p[g % 2][:, :], ones_col_bf, xsq[c % 2][:, :],
                                start=(c == 0), stop=(c == NCH - 1))
                inst.then_inc(SEM["pe"], 1)
                assert sm.bump("pe") == PE_ST(g, c)

        def emit_bcast(t, g):
            t.wait_ge(SEM["sc"], SC_RSTD(g))
            if g == 0:
                t.wait_ge(SEM["dv"], DV_MCOL)
            if g > 0:
                t.wait_ge(SEM["dv"], DV_RC(g - 1))
            t.matmul(pm[:, :], onesf[:, :], s1row[:, :], start=True, stop=True)
            inst = t.matmul(pr[:, :], onesf[:, :], s2row[:, :], start=True, stop=True)
            inst.then_inc(SEM["pe"], 1)
            assert sm.bump("pe") == PE_BC(g)

        @blk.tensor
        def _(t):
            for g in range(NG):
                emit_stats(t, g)
                if g >= 1:
                    emit_bcast(t, g - 1)
            emit_bcast(t, NG - 1)

        @blk.vector
        def _(v):
            for g in range(NG):
                v.wait_ge(SEM["pe"], PE_ST(g, NCH - 1))
                fl(v, "dvf", v.tensor_copy(s1row[:, :], s1p[g % 2][:, :]))
                fl(v, "dvf", v.tensor_copy(s2row[:, :], s2p[g % 2][:, :]))
                fl(v, "dvf", v.tensor_scalar_mul(s1row[:, :], s1row[:, :], -1.0 / H))
                fl(v, "dvf", v.tensor_scalar_mul(s2row[:, :], s2row[:, :], 1.0 / H))
                fl(v, "dvf", v.tensor_mul(m2row[:, :], s1row[:, :], s1row[:, :]))
                fl(v, "dvf", v.tensor_sub(s2row[:, :], s2row[:, :], m2row[:, :]))
                v.tensor_scalar_add(s2row[:, :], s2row[:, :], EPS).then_inc(SEM["dv"], 1)
                assert sm.bump("dv") == DV_VAR(g)
                v.wait_ge(SEM["pe"], PE_BC(g))
                v.tensor_copy(nm128[g % 2][:, :], pm[:, :]).then_inc(SEM["dv"], 1)
                assert sm.bump("dv") == DV_NM(g)
                v.tensor_copy(r128[g % 2][:, :], pr[:, :]).then_inc(SEM["dv"], 1)
                assert sm.bump("dv") == DV_RC(g)
                for c in range(NCH):
                    sl4 = c // 4
                    if g * 8 + sl4 >= 2:
                        pg8, psl = divmod(g * 8 + sl4 - 2, 8)
                        v.wait_ge(SEM["qld"], QLD_ST8(pg8, psl))
                    fl(v, "dvf", v.tensor_tensor(tmpf[c % 4][:, :],
                                                 xbg[g % 3][:, c * GT:(c + 1) * GT],
                                                 nm128[g % 2][:, :], ALU.add))
                    v.tensor_tensor(xnb[(g * 8 + sl4) % 2][:, (c % 4) * GT:(c % 4 + 1) * GT],
                                    tmpf[c % 4][:, :], r128[g % 2][:, :],
                                    ALU.mult).then_inc(SEM["dv"], 1)
                    assert sm.bump("dv") == DV_MULT(g, c)

        @blk.sync
        def _(sy):
            for g in range(NG):
                p, gI = gsl(g)
                for sl in range(8):
                    sy.wait_ge(SEM["dv"], DV_MULT(g, sl * 4 + 3))
                    t = dma(sy, "qld",
                            xns[p, :, gI, sl * 4:(sl + 1) * 4, :],
                            xnb[(g * 8 + sl) % 2].ap().rearrange("p (c t) -> p c t", t=GT))
                    assert t == QLD_ST8(g, sl)

    WSB = ptop("WSB", [128, 32768], BF16)

    b_qw, b_qld, b_pe, b_sc, b_qst = (sm.val(n) for n in ("qw", "qld", "pe", "sc", "qst"))
    NTILE = 8
    QW_WV = b_qw + 32
    QLD_Q4 = lambda T, q: b_qld + 16 * (T * 4 + q + 1)
    FLA1 = b_qld + 16
    PE_X = lambda T, c: b_pe + T * NCH + c + 1
    SC_E = lambda T, oc: b_sc + T * 8 + oc + 1
    QST_S = lambda T, oc: b_qst + 16 * (T * 8 + oc + 1)
    A1_KS_DONE = b_qst + 16 * 64

    def q4wait(T, c):
        q = min((c + 4) // 8, 3)
        if T == 0:
            q = max(q, 1)
        return QLD_Q4(T, q)

    with contextlib.ExitStack() as ph, nc.Block(no_gpsimd_drain=True) as blk:
        E = ph.enter_context
        wsb = WSA
        xq = [E(nc.sbuf_tensor(f"xqa{i}", [128, 8 * TT], BF16)) for i in range(4)]
        ksb = [E(nc.sbuf_tensor(f"ksb{i}", [128, 512], BF16)) for i in range(2)]
        pk = [E(nc.psum_tensor(f"pk{i}", [128, 512], F32)) for i in range(8)]

        @blk.gpsimd
        def _(g):
            for h in range(2):
                dma(g, "qw",
                    WSB.ap().rearrange("p (c o) -> p c o", o=D)[:, h * 16:(h + 1) * 16, :],
                    wv[:, h * 16:(h + 1) * 16, :])
            assert sm.val("qw") == QW_WV

        @blk.sync
        def _(sy):
            for T in range(NTILE):
                p, gI = T // 4, T % 4
                for q in range(4):
                    u = T * 4 + q
                    if u >= 4:
                        pu = u - 4
                        sy.wait_ge(SEM["pe"], PE_X(*divmod(pu * 8 + 7, NCH)))
                    t = dma(sy, "qld", xq[u % 4].ap().rearrange("p (c t) -> p c t", t=TT),
                            xns[p, :, gI, q * 8:(q + 1) * 8, :])
                    assert t == QLD_Q4(T, q)

        @blk.tensor
        def _(t):
            t.wait_ge(SEM["qw"], QW_WK)
            for T in range(NTILE):
                for c in range(NCH):
                    t.wait_ge(SEM["qld"], q4wait(T, c))
                    if T > 0 and c == 0:
                        t.wait_ge(SEM["sc"], SC_E(T - 1, 7))
                    u4 = (T * 4 + c // 8) % 4
                    for oc in range(8):
                        inst = t.matmul(pk[oc][:, :],
                                        wsb[:, c * D + oc * 128: c * D + (oc + 1) * 128],
                                        xq[u4][:, (c % 8) * TT:(c % 8 + 1) * TT],
                                        start=(c == 0), stop=(c == NCH - 1))
                    inst.then_inc(SEM["pe"], 1)
                    assert sm.bump("pe") == PE_X(T, c)

        @blk.scalar
        def _(s):
            for T in range(NTILE):
                s.wait_ge(SEM["pe"], PE_X(T, NCH - 1))
                for oc in range(8):
                    k = T * 8 + oc
                    if k >= 2:
                        pT, poc = divmod(k - 2, 8)
                        s.wait_ge(SEM["qst"], QST_S(pT, poc))
                    s.activation(ksb[k % 2][:, :], pk[oc][:, :], AF.Identity,
                                 bias=bkc[:, oc:oc + 1]).then_inc(SEM["sc"], 1)
                    assert sm.bump("sc") == SC_E(T, oc)
                    s.wait_ge(SEM["sc"], SC_E(T, oc))
                    t2 = dma(s, "qst", ks[:, oc, T * TT:(T + 1) * TT], ksb[k % 2][:, :])
                    assert t2 == QST_S(T, oc)
            s.wait_ge(SEM["qst"], A1_KS_DONE)

    acsb = ptop("ac_sb", [128, NCH * 48], BF16)

    b_qw, b_qld, b_pe, b_dv, b_qst = (sm.val(n) for n in ("qw", "qld", "pe", "dv", "qst"))
    QW_BT = b_qw + 16 * 3
    QW_WQ = b_qw + 16 * 6
    PE_BV = b_pe + 1
    DV_BV = b_dv + 1
    DV_BTOT = b_dv + 2
    QLD_Q4 = lambda T, q: b_qld + 16 * (T * 4 + q + 1)
    PE_X = lambda T, c: PE_BV + T * NCH + c + 1
    DV_E = lambda T, sub: DV_BTOT + T * 4 + sub + 1
    QST_S = lambda T, sub: b_qst + 16 * (T * 4 + sub + 1)
    A2_VS_DONE = b_qst + 16 * 32

    def q4wait2(T, c):
        q = min((c + 4) // 8, 3)
        if T == 0:
            q = max(q, 1)
        return QLD_Q4(T, q)

    with contextlib.ExitStack() as ph, nc.Block(no_gpsimd_drain=True) as blk:
        E = ph.enter_context
        wsb = WSB
        xq = [E(nc.sbuf_tensor(f"xqb{i}", [128, 8 * TT], BF16)) for i in range(4)]
        vsb = [E(nc.sbuf_tensor(f"vsb{i}", [128, D], BF16)) for i in range(2)]
        bvrow = E(nc.sbuf_tensor("bvrow", [1, D], F32))
        bv128 = E(nc.sbuf_tensor("bv128", [128, D], F32))
        borow = E(nc.sbuf_tensor("borow", [1, H], BF16))
        bfrow = E(nc.sbuf_tensor("bfrow", [1, H], BF16))
        pv = [E(nc.psum_tensor(f"pv{i}", [128, 512], F32)) for i in range(8)]

        @blk.gpsimd
        def _(g):
            dma(g, "qw", bvrow[:, :], bv_d)
            dma(g, "qw", borow[:, :], bo_d)
            dma(g, "qw", bfrow[:, :], bf_d)
            assert sm.val("qw") == QW_BT
            for h in range(2):
                dma(g, "qw",
                    WSA.ap().rearrange("p (c o) -> p c o", o=D)[:, h * 16:(h + 1) * 16, :],
                    wq[:, h * 16:(h + 1) * 16, :])
            dma(g, "qw", acsb.ap().rearrange("p (c r) -> p c r", r=48), acat)
            assert sm.val("qw") == QW_WQ

        @blk.tensor
        def _(t):
            t.wait_ge(SEM["qw"], b_qw + 16)
            for h in range(2):
                inst = t.matmul(pv[h][:, :], onesf[:, :], bvrow[:, h * 512:(h + 1) * 512],
                                start=True, stop=True)
            inst.then_inc(SEM["pe"], 1)
            assert sm.bump("pe") == PE_BV

        @blk.vector
        def _(v):
            v.wait_ge(SEM["pe"], PE_BV)
            fl(v, "dvf", v.tensor_copy(bv128[:, 0:512], pv[0][:, :]))
            v.tensor_copy(bv128[:, 512:1024], pv[1][:, :]).then_inc(SEM["dv"], 1)
            assert sm.bump("dv") == DV_BV
            v.wait_ge(SEM["qw"], QW_BT)
            v.tensor_tensor(btot[:, :], borow[:, :], bfrow[:, :], ALU.add).then_inc(SEM["dv"], 1)
            assert sm.bump("dv") == DV_BTOT

        @blk.sync
        def _(sy):
            for T in range(NTILE):
                p, gI = T // 4, T % 4
                for q in range(4):
                    u = T * 4 + q
                    if u >= 4:
                        pu = u - 4
                        sy.wait_ge(SEM["pe"], PE_X(*divmod(pu * 8 + 7, NCH)))
                    t = dma(sy, "qld", xq[u % 4].ap().rearrange("p (c t) -> p c t", t=TT),
                            xns[p, :, gI, q * 8:(q + 1) * 8, :])
                    assert t == QLD_Q4(T, q)

        @blk.tensor
        def _(t):
            t.wait_ge(SEM["dv"], DV_BV)
            t.wait_ge(SEM["qw"], QW_WV)
            for T in range(NTILE):
                for c in range(NCH):
                    t.wait_ge(SEM["qld"], q4wait2(T, c))
                    if T > 0 and c == 0:
                        t.wait_ge(SEM["dv"], DV_E(T - 1, 3))
                    u4 = (T * 4 + c // 8) % 4
                    for sub in range(4):
                        for h in range(2):
                            inst = t.matmul(pv[sub * 2 + h][:, :],
                                            xq[u4][:, (c % 8) * TT + sub * 128:
                                                   (c % 8) * TT + (sub + 1) * 128],
                                            wsb[:, c * D + h * 512: c * D + (h + 1) * 512],
                                            start=(c == 0), stop=(c == NCH - 1))
                    inst.then_inc(SEM["pe"], 1)
                    assert sm.bump("pe") == PE_X(T, c)

        @blk.vector
        def _(v):
            for T in range(NTILE):
                v.wait_ge(SEM["pe"], PE_X(T, NCH - 1))
                for sub in range(4):
                    k = T * 4 + sub
                    if k >= 2:
                        pT, psub = divmod(k - 2, 4)
                        v.wait_ge(SEM["qst"], QST_S(pT, psub))
                    fl(v, "dvf", v.tensor_tensor(vsb[k % 2][:, 0:512], pv[sub * 2][:, :],
                                                 bv128[:, 0:512], ALU.add))
                    v.tensor_tensor(vsb[k % 2][:, 512:1024], pv[sub * 2 + 1][:, :],
                                    bv128[:, 512:1024], ALU.add).then_inc(SEM["dv"], 1)
                    assert sm.bump("dv") == DV_E(T, sub)

        @blk.scalar
        def _(s):
            for T in range(NTILE):
                for sub in range(4):
                    k = T * 4 + sub
                    s.wait_ge(SEM["dv"], DV_E(T, sub))
                    t2 = dma(s, "qst", vs[:, k, :], vsb[k % 2][:, :])
                    assert t2 == QST_S(T, sub)
            s.wait_ge(SEM["qst"], A2_VS_DONE)

    b_qld, b_pe, b_dv, b_sc, b_qst = (sm.val(n) for n in ("qld", "pe", "dv", "sc", "qst"))
    NT3 = 8
    T3 = 256
    QLD_X = lambda T, q: b_qld + 16 * (T * 4 + q + 1) + (16 if T >= 1 else 0) + (32 if T >= 2 else 0)
    FLT0 = b_qld + 16 * 5
    QLD_K = b_qld + 16 * 10
    FLK = b_qld + 16 * 11
    PE_W = lambda T, w: b_pe + T * 2 + w + 1
    SC_Q = lambda T, oc: b_sc + T * 9 + oc + 1
    SC_R = lambda T: b_sc + T * 9 + 9
    QST_Q = lambda T, oc: b_qst + 16 * (T * 8 + oc + 1)
    A3_QS_DONE = b_qst + 16 * 64

    with contextlib.ExitStack() as ph, nc.Block(no_gpsimd_drain=True) as blk:
        E = ph.enter_context
        wsb = WSA
        xnt = [E(nc.sbuf_tensor(f"xnt{i}", [128, NCH * T3], BF16)) for i in range(2)]
        qsb = [E(nc.sbuf_tensor(f"qsb{i}", [128, T3], BF16)) for i in range(2)]
        pq = [E(nc.psum_tensor(f"pq{i}", [128, T3], F32)) for i in range(4)]
        prr = E(nc.psum_tensor("prr", [48, T3], F32))

        @blk.sync
        def _(sy):
            for T in range(NT3):
                if T >= 2:
                    sy.wait_ge(SEM["pe"], PE_W(T - 2, 1))
                gI, half = T // 2, T % 2
                for q in range(4):
                    t = dma(sy, "qld",
                            xnt[T % 2].ap().rearrange("p (c t) -> p c t", t=T3)[:, q * 8:(q + 1) * 8, :],
                            xns[0, :, gI, q * 8:(q + 1) * 8, half * T3:(half + 1) * T3])
                    assert t == QLD_X(T, q)
                if T == 0:
                    assert dma(sy, "qld", flsh, xnt[0][:, 0:2]) == FLT0
                if T == 1:
                    sy.wait_ge(SEM["qst"], A1_KS_DONE)
                    t = dma(sy, "qld", WSB.ap().rearrange("p (a t) -> p a t", t=S), ks)
                    assert t == QLD_K
                    assert dma(sy, "qld", flsh, WSB[:, 0:2]) == FLK

        @blk.tensor
        def _(t):
            t.wait_ge(SEM["qw"], QW_WQ)
            for T in range(NT3):
                t.wait_ge(SEM["qld"], FLT0 if T == 0 else QLD_X(T, 3))
                for w in range(2):
                    for c in range(NCH):
                        for i in range(4):
                            oc = w * 4 + i
                            if c == 0 and (T > 0 or w > 0):
                                pT, pw = (T, w - 1) if w > 0 else (T - 1, 1)
                                t.wait_ge(SEM["sc"], SC_Q(pT, pw * 4 + i))
                            inst = t.matmul(pq[i][:, :],
                                            wsb[:, c * D + oc * 128: c * D + (oc + 1) * 128],
                                            xnt[T % 2][:, c * T3:(c + 1) * T3],
                                            start=(c == 0), stop=(c == NCH - 1))
                        if w == 1:
                            if c == 0 and T > 0:
                                t.wait_ge(SEM["sc"], SC_R(T - 1))
                            inst = t.matmul(prr[:, :], acsb[:, c * 48:(c + 1) * 48],
                                            xnt[T % 2][:, c * T3:(c + 1) * T3],
                                            start=(c == 0), stop=(c == NCH - 1))
                    inst.then_inc(SEM["pe"], 1)
                    assert sm.bump("pe") == PE_W(T, w)

        @blk.scalar
        def _(s):
            for T in range(NT3):
                for w in range(2):
                    s.wait_ge(SEM["pe"], PE_W(T, w))
                    for i in range(4):
                        oc = w * 4 + i
                        k = T * 8 + oc
                        if k >= 2:
                            pT, poc = divmod(k - 2, 8)
                            s.wait_ge(SEM["qst"], QST_Q(pT, poc))
                        s.activation(qsb[k % 2][:, :], pq[i][:, :], AF.Identity,
                                     bias=bq32[:, oc:oc + 1],
                                     scale=1.0 / 32.0).then_inc(SEM["sc"], 1)
                        assert sm.bump("sc") == SC_Q(T, oc)
                        s.wait_ge(SEM["sc"], SC_Q(T, oc))
                        t2 = dma(s, "qst",
                                 qs[T * 2:(T + 1) * 2, :, oc, :].rearrange("q p t -> p q t"),
                                 qsb[k % 2].ap().rearrange("p (q t) -> p q t", t=128))
                        assert t2 == QST_Q(T, oc)
                    if w == 1:
                        s.wait_ge(SEM["dv"], DV_MCOL)
                        s.activation(rsb[:, T * T3:(T + 1) * T3], prr[:, :], AF.Identity,
                                     scale=mcol[:, :]).then_inc(SEM["sc"], 1)
                        assert sm.bump("sc") == SC_R(T)
            s.wait_ge(SEM["qst"], A3_QS_DONE)

    b_qld, b_pe, b_dv, b_sc, b_qst = (sm.val(n) for n in ("qld", "pe", "dv", "sc", "qst"))
    NQT = 16
    FLQ0 = b_qld + 32
    QLD_V = b_qld + 48
    FLV = b_qld + 64
    QLD_Q = lambda qt: b_qld + 64 + 16 * qt
    PE_S = lambda qt, kb: b_pe + qt * 19 + kb + 1
    PE_PTG = lambda qt, g: b_pe + qt * 19 + 8 + g + 1
    PE_PV = lambda qt: b_pe + qt * 19 + 17
    PE_OTG = lambda qt, g2: b_pe + qt * 19 + 17 + g2 + 1
    DV_BMX = lambda qt, kb: b_dv + qt * 21 + kb + 1
    DV_NMX = lambda qt: b_dv + qt * 21 + 9
    DV_RL = lambda qt: b_dv + qt * 21 + 10
    DV_PTC = lambda qt, g: b_dv + qt * 21 + 10 + g + 1
    DV_OEV = lambda qt: b_dv + qt * 21 + 19
    DV_OTC = lambda qt, g2: b_dv + qt * 21 + 19 + g2 + 1
    SC_SE = lambda qt, kb: b_sc + qt * 9 + kb + 1
    SC_EXP = lambda qt: b_sc + qt * 9 + 9
    QST_OT = lambda qt: b_qst + 16 * (qt + 1)
    B_OT_DONE = b_qst + 16 * 16

    def DV_PT_USE(u):
        q, r = divmod(u, 10)
        return DV_PTC(q, r) if r < 8 else DV_OTC(q, r - 8)

    with contextlib.ExitStack() as ph, nc.Block(no_gpsimd_drain=True) as blk:
        E = ph.enter_context
        k_sb = WSB
        v_sb = WSA
        qtb = [E(nc.sbuf_tensor(f"qtb{i}", [128, 8 * 128], BF16)) for i in range(2)]
        s_sb = E(nc.sbuf_tensor("s_sb", [128, S], F32))
        p_sb = E(nc.sbuf_tensor("p_sb", [128, S], BF16))
        pt_sb = E(nc.sbuf_tensor("pt_sb", [128, NCH * 128], BF16))
        osb = E(nc.sbuf_tensor("osb", [128, D], BF16))
        otb = [E(nc.sbuf_tensor(f"otb{i}", [128, D], BF16)) for i in range(2)]
        nmx = E(nc.sbuf_tensor("nmx", [128, 1], F32))
        bmx = E(nc.sbuf_tensor("bmx", [128, 8], F32))
        lsum = E(nc.sbuf_tensor("lsum", [128, 1], F32))
        rl = E(nc.sbuf_tensor("rl", [128, 1], F32))
        ps = [E(nc.psum_tensor(f"ps{i}", [128, 512], F32)) for i in range(2)]
        pt = [E(nc.psum_tensor(f"pt{i}", [128, 512], BF16)) for i in range(2)]
        po = [E(nc.psum_tensor(f"po{i}", [128, 512], F32)) for i in range(2)]

        @blk.sync
        def _(sy):
            sy.wait_ge(SEM["qst"], A3_QS_DONE)
            dma(sy, "qld", qtb[0].ap().rearrange("p (a t) -> p a t", t=128), qs[0])
            assert dma(sy, "qld", flsh, qtb[0][:, 0:2]) == FLQ0
            sy.wait_ge(SEM["qst"], A2_VS_DONE)
            t = dma(sy, "qld", v_sb.ap().rearrange("p (a t) -> p a t", t=D), vs)
            assert t == QLD_V
            assert dma(sy, "qld", flsh, v_sb[:, 0:2]) == FLV
            for qt in range(1, NQT):
                if qt >= 2:
                    sy.wait_ge(SEM["pe"], PE_S(qt - 2, 7))
                t = dma(sy, "qld",
                        qtb[qt % 2].ap().rearrange("p (a t) -> p a t", t=128), qs[qt])
                assert t == QLD_Q(qt)

        @blk.tensor
        def _(t):
            t.wait_ge(SEM["qld"], FLK)
            ptu = 0
            for qt in range(NQT):
                t.wait_ge(SEM["qld"], FLQ0 if qt == 0 else QLD_Q(qt))
                for kb in range(8):
                    k = qt * 8 + kb
                    if k >= 2:
                        pqt, pkb = divmod(k - 2, 8)
                        t.wait_ge(SEM["sc"], SC_SE(pqt, pkb))
                    for oc in range(8):
                        inst = t.matmul(ps[kb % 2][:, :],
                                        qtb[qt % 2][:, oc * 128:(oc + 1) * 128],
                                        k_sb[:, oc * S + kb * 512: oc * S + (kb + 1) * 512],
                                        start=(oc == 0), stop=(oc == 7))
                    inst.then_inc(SEM["pe"], 1)
                    assert sm.bump("pe") == PE_S(qt, kb)
                t.wait_ge(SEM["sc"], SC_EXP(qt))
                for g in range(8):
                    if ptu >= 2:
                        t.wait_ge(SEM["dv"], DV_PT_USE(ptu - 2))
                    for i in range(4):
                        kc = g * 4 + i
                        inst = t.transpose(pt[ptu % 2][:, i * 128:(i + 1) * 128],
                                           p_sb[:, kc * 128:(kc + 1) * 128], ident[:, :])
                    inst.then_inc(SEM["pe"], 1)
                    assert sm.bump("pe") == PE_PTG(qt, g)
                    ptu += 1
                if qt == 0:
                    t.wait_ge(SEM["qld"], FLV)
                else:
                    t.wait_ge(SEM["dv"], DV_OEV(qt - 1))
                for kc in range(NCH):
                    t.wait_ge(SEM["dv"], DV_PTC(qt, kc // 4))
                    for h in range(2):
                        inst = t.matmul(po[h][:, :],
                                        pt_sb[:, kc * 128:(kc + 1) * 128],
                                        v_sb[:, kc * D + h * 512: kc * D + (h + 1) * 512],
                                        start=(kc == 0), stop=(kc == NCH - 1))
                inst.then_inc(SEM["pe"], 1)
                assert sm.bump("pe") == PE_PV(qt)
                t.wait_ge(SEM["dv"], DV_OEV(qt))
                for g2 in range(2):
                    if ptu >= 2:
                        t.wait_ge(SEM["dv"], DV_PT_USE(ptu - 2))
                    for i in range(4):
                        dc = g2 * 4 + i
                        inst = t.transpose(pt[ptu % 2][:, i * 128:(i + 1) * 128],
                                           osb[:, dc * 128:(dc + 1) * 128], ident[:, :])
                    inst.then_inc(SEM["pe"], 1)
                    assert sm.bump("pe") == PE_OTG(qt, g2)
                    ptu += 1

        @blk.scalar
        def _(s):
            for qt in range(NQT):
                for kb in range(8):
                    s.wait_ge(SEM["pe"], PE_S(qt, kb))
                    s.activation(s_sb[:, kb * 512:(kb + 1) * 512], ps[kb % 2][:, :],
                                 AF.Copy).then_inc(SEM["sc"], 1)
                    assert sm.bump("sc") == SC_SE(qt, kb)
                s.wait_ge(SEM["dv"], DV_NMX(qt))
                s.activation(p_sb[:, :], s_sb[:, :], AF.Exp, bias=nmx[:, :],
                             accum_out=lsum[:, :]).then_inc(SEM["sc"], 1)
                assert sm.bump("sc") == SC_EXP(qt)
                s.wait_ge(SEM["sc"], SC_EXP(qt))

        @blk.vector
        def _(v):
            ptu = 0
            for qt in range(NQT):
                for kb in range(8):
                    v.wait_ge(SEM["sc"], SC_SE(qt, kb))
                    v.reduce_max(bmx[:, kb:kb + 1], s_sb[:, kb * 512:(kb + 1) * 512],
                                 axis=AX.X).then_inc(SEM["dv"], 1)
                    assert sm.bump("dv") == DV_BMX(qt, kb)
                fl(v, "dvf", v.engine_nop())
                v.reduce_max(nmx[:, :], bmx[:, :], axis=AX.X, negate=True).then_inc(SEM["dv"], 1)
                assert sm.bump("dv") == DV_NMX(qt)
                v.wait_ge(SEM["sc"], SC_EXP(qt))
                v.reciprocal(rl[:, :], lsum[:, :]).then_inc(SEM["dv"], 1)
                assert sm.bump("dv") == DV_RL(qt)
                for g in range(8):
                    v.wait_ge(SEM["pe"], PE_PTG(qt, g))
                    v.tensor_copy(pt_sb[:, g * 512:(g + 1) * 512],
                                  pt[ptu % 2][:, :]).then_inc(SEM["dv"], 1)
                    assert sm.bump("dv") == DV_PTC(qt, g)
                    ptu += 1
                v.wait_ge(SEM["pe"], PE_PV(qt))
                if qt > 0:
                    v.wait_ge(SEM["pe"], PE_OTG(qt - 1, 1))
                fl(v, "dvf", v.tensor_scalar_mul(osb[:, 0:512], po[0][:, :], rl[:, :]))
                v.tensor_scalar_mul(osb[:, 512:1024], po[1][:, :], rl[:, :]).then_inc(SEM["dv"], 1)
                assert sm.bump("dv") == DV_OEV(qt)
                for g2 in range(2):
                    v.wait_ge(SEM["pe"], PE_OTG(qt, g2))
                    if qt >= 2 and g2 == 0:
                        v.wait_ge(SEM["qst"], QST_OT(qt - 2))
                    v.tensor_copy(otb[qt % 2][:, g2 * 512:(g2 + 1) * 512],
                                  pt[ptu % 2][:, :]).then_inc(SEM["dv"], 1)
                    assert sm.bump("dv") == DV_OTC(qt, g2)
                    ptu += 1

        @blk.gpsimd
        def _(g):
            for qt in range(NQT):
                g.wait_ge(SEM["dv"], DV_OTC(qt, 1))
                t = dma(g, "qst", ots[:, qt, :, :].rearrange("p a t -> p (a t)"), otb[qt % 2][:, :])
                assert t == QST_OT(qt)

    b_qw, b_qld, b_pe, b_dv, b_qst = (sm.val(n) for n in ("qw", "qld", "pe", "dv", "qst"))
    NJ = 8
    QW_WJ = lambda j: b_qw + 64 * (j + 1)
    QLD_GRP = lambda j, gI: b_qld + 16 * ((j * 4 + gI) * 3 + 3)
    PE_T = lambda j, ti: b_pe + j * 16 + ti + 1
    DV_T = lambda j, ti: b_dv + j * 16 + ti + 1
    QST_T = lambda j, ti: b_qst + 16 * (j * 16 + ti + 1)

    with contextlib.ExitStack() as ph, nc.Block(no_gpsimd_drain=True) as blk:
        E = ph.enter_context
        wfsb = [WSA[:, 0:16384], WSA[:, 16384:32768]]
        xng = [WSB[:, 0:16384], WSB[:, 16384:32768]]
        wosb = [E(nc.sbuf_tensor(f"wo_sb{i}", [128, 8 * 512], BF16)) for i in range(2)]
        bcsb = [E(nc.sbuf_tensor(f"bc_sb{i}", [48, 512], BF16)) for i in range(2)]
        otg = [E(nc.sbuf_tensor(f"otg{i}", [128, 4 * 1024], BF16)) for i in range(2)]
        xrg = [E(nc.sbuf_tensor(f"xrg{i}", [128, 4 * 512], F32)) for i in range(2)]
        outb = [E(nc.sbuf_tensor(f"outb{i}", [128, 512], F32)) for i in range(2)]
        pc = [E(nc.psum_tensor(f"pc{i}", [128, 512], F32)) for i in range(2)]

        @blk.gpsimd
        def _(g):
            for j in range(NJ):
                if j >= 2:
                    g.wait_ge(SEM["pe"], PE_T(j - 2, 15))
                dma(g, "qw", wfsb[j % 2].rearrange("p (c o) -> p c o", o=512), wf[j])
                dma(g, "qw", wosb[j % 2].ap().rearrange("p (c o) -> p c o", o=512), wo[j])
                dma(g, "qw", bcsb[j % 2][:, :], bcat[j])
                t = dma(g, "qw", flsh, wfsb[j % 2][:, 0:2])
                assert t == QW_WJ(j)

        @blk.sync
        def _(sy):
            sy.wait_ge(SEM["qst"], B_OT_DONE)
            for j in range(NJ):
                for gI in range(4):
                    gi = j * 4 + gI
                    if gi >= 2:
                        pj, pg = divmod(gi - 2, 4)
                        sy.wait_ge(SEM["dv"], DV_T(pj, pg * 4 + 3))
                    dma(sy, "qld", xng[gi % 2].rearrange("p (c t) -> p c t", t=512),
                        xns[0, :, gI, :, :])
                    dma(sy, "qld", otg[gi % 2].ap().rearrange("p (q a t) -> p q a t", a=8, t=128),
                        ots[:, gI * 4:(gI + 1) * 4, :, :])
                    t = dma(sy, "qld", xrg[gi % 2].ap().rearrange("p (q t) -> p q t", t=512),
                            x_own.rearrange("(q p) o -> p q o", p=128)[:, gI * 4:(gI + 1) * 4,
                                                                      j * 512:(j + 1) * 512])
                    assert t == QLD_GRP(j, gI)

        @blk.tensor
        def _(t):
            t.wait_ge(SEM["dv"], DV_BTOT)
            for j in range(NJ):
                t.wait_ge(SEM["qw"], QW_WJ(j))
                for gI in range(4):
                    gi = j * 4 + gI
                    t.wait_ge(SEM["qld"], QLD_GRP(j, gI))
                    for tt in range(4):
                        ti = gI * 4 + tt
                        k = j * 16 + ti
                        if k >= 2:
                            pj, pti = divmod(k - 2, 16)
                            t.wait_ge(SEM["dv"], DV_T(pj, pti))
                        for c in range(NCH):
                            t.matmul(pc[k % 2][:, :],
                                     xng[gi % 2][:, c * 512 + tt * 128: c * 512 + (tt + 1) * 128],
                                     wfsb[j % 2][:, c * 512:(c + 1) * 512],
                                     start=(c == 0), stop=False)
                        for dc in range(8):
                            t.matmul(pc[k % 2][:, :],
                                     otg[gi % 2][:, (tt * 8 + dc) * 128:(tt * 8 + dc + 1) * 128],
                                     wosb[j % 2][:, dc * 512:(dc + 1) * 512],
                                     start=False, stop=False)
                        t.matmul(pc[k % 2][:, :], rsb[:, ti * 128:(ti + 1) * 128],
                                 bcsb[j % 2][:, :], start=False, stop=False)
                        inst = t.matmul(pc[k % 2][:, :], onesb[:, :],
                                        btot[:, j * 512:(j + 1) * 512], start=False, stop=True)
                        inst.then_inc(SEM["pe"], 1)
                        assert sm.bump("pe") == PE_T(j, ti)

        @blk.vector
        def _(v):
            for j in range(NJ):
                for ti in range(16):
                    k = j * 16 + ti
                    v.wait_ge(SEM["pe"], PE_T(j, ti))
                    if k >= 2:
                        pj, pti = divmod(k - 2, 16)
                        v.wait_ge(SEM["qst"], QST_T(pj, pti))
                    v.tensor_tensor(outb[k % 2][:, :], pc[k % 2][:, :],
                                    xrg[(ti // 4 + j * 4) % 2][:, (ti % 4) * 512:(ti % 4 + 1) * 512],
                                    ALU.add).then_inc(SEM["dv"], 1)
                    assert sm.bump("dv") == DV_T(j, ti)

        @blk.scalar
        def _(s):
            for j in range(NJ):
                for ti in range(16):
                    k = j * 16 + ti
                    s.wait_ge(SEM["dv"], DV_T(j, ti))
                    t2 = dma(s, "qst",
                             out_d[ti * 128:(ti + 1) * 128, j * 512:(j + 1) * 512],
                             outb[k % 2][:, :])
                    assert t2 == QST_T(j, ti)
            s.wait_ge(SEM["qst"], sm.val("qst"))

    top.close()
    sm.ctx.close()
    return nc


def kernel(L0_attn_x, L0_attn_Wqkv, L0_attn_bqkv, L0_attn_Wo, L0_attn_bo,
           L0_ffn_weight, L0_ffn_bias, L0_ffn_lora_A, L0_ffn_lora_B,
           L0_ffn_lora_start_indices):
    x = np.asarray(L0_attn_x, dtype=np.float32)
    Wqkv = np.asarray(L0_attn_Wqkv, dtype=np.float32)
    bqkv = np.asarray(L0_attn_bqkv, dtype=np.float32)
    Wo = np.asarray(L0_attn_Wo, dtype=np.float32)
    bo = np.asarray(L0_attn_bo, dtype=np.float32)
    Wff = np.asarray(L0_ffn_weight, dtype=np.float32)
    bff = np.asarray(L0_ffn_bias, dtype=np.float32)
    lA = np.asarray(L0_ffn_lora_A, dtype=np.float32)
    lB = np.asarray(L0_ffn_lora_B, dtype=np.float32)
    lsi = np.asarray(L0_ffn_lora_start_indices, dtype=np.int32)

    bf = ml_dtypes.bfloat16

    def pmaj(w):
        o = w.shape[1]
        return np.ascontiguousarray(w.reshape(NCH, 128, o).transpose(1, 0, 2))

    wq_h = pmaj(Wqkv[:, 0:D])
    wk_h = pmaj(Wqkv[:, D:2 * D])
    wv_h = pmaj(Wqkv[:, 2 * D:3 * D])
    wf_h = np.ascontiguousarray(Wff.T.reshape(NCH, 128, 8, 512).transpose(2, 1, 0, 3))
    wo_h = np.ascontiguousarray(Wo.reshape(8, 128, 8, 512).transpose(2, 1, 0, 3))
    acat_h = np.ascontiguousarray(lA.reshape(48, H).T.reshape(NCH, 128, 48).transpose(1, 0, 2))
    bcat_h = np.ascontiguousarray(lB.transpose(0, 2, 1).reshape(48, 8, 512).transpose(1, 0, 2))
    bqc_h = np.ascontiguousarray(bqkv[0:D].reshape(8, 128).T)
    bkc_h = np.ascontiguousarray(bqkv[D:2 * D].reshape(8, 128).T)
    bv_h = np.ascontiguousarray(bqkv[2 * D:3 * D][None, :])
    bo_h = np.ascontiguousarray(bo[None, :])
    bf_h = np.ascontiguousarray(bff[None, :])
    lsi_h = np.ascontiguousarray(lsi[None, :])
    idxp_h = np.repeat(np.arange(3, dtype=np.float32), 16)[None, :].copy()
    onesf_h = np.ones((1, 128), dtype=np.float32)
    onesb_h = np.ones((1, 128), dtype=bf)
    ident_h = np.eye(128, dtype=bf)

    xt_half = {}
    for b in range(B):
        for h in range(2):
            xt_half[(b, h)] = np.ascontiguousarray(
                x[b, h * SOWN:(h + 1) * SOWN, :].T.reshape(NCH, 128, 4, GT).transpose(1, 2, 0, 3))

    in_maps = []
    for c in range(8):
        b, h = c // 2, c % 2
        in_maps.append(dict(
            xt0=xt_half[(b, h)], xt1=xt_half[(b, 1 - h)],
            x_own=np.ascontiguousarray(x[b, h * SOWN:(h + 1) * SOWN, :]),
            wq=wq_h, wk=wk_h, wv=wv_h, wf=wf_h, wo=wo_h,
            acat=acat_h, bcat=bcat_h,
            bqc=bqc_h, bkc=bkc_h, bv_row=bv_h, bo_row=bo_h, bf_row=bf_h,
            lsi=lsi_h, bvec=np.array([[float(b)]], dtype=np.float32),
            idxp=idxp_h, onesf=onesf_h, onesb=onesb_h, ident=ident_h,
        ))

    nc = build_nc()
    res = run_bass_kernel_spmd(nc, in_maps, core_ids=list(range(8)))

    out = np.empty((B, S, H), dtype=np.float32)
    for c in range(8):
        b, h = c // 2, c % 2
        out[b, h * SOWN:(h + 1) * SOWN, :] = res.results[c]["out"]
    return out
